# version 04 (final)
# baseline (speedup 1.0000x reference)
import numpy as np
import concourse.bass as bass
import concourse.mybir as mybir
from concourse.bass_utils import run_bass_kernel_spmd

F32 = mybir.dt.float32
BF16 = mybir.dt.bfloat16
AF = mybir.ActivationFunctionType
ALU = mybir.AluOpType
AX = mybir.AxisListType

B, T, D = 2, 8192, 2048
NH, HD = 16, 128
NE, NSH, FF = 16, 2, 1024
NEX = NE + NSH
TOK = 2048
DC = D // 128
FC = FF // 128
KCN = T // 128
NTC = T // 512
NQC = TOK // 512
EPS = 1e-5
ISQ = float(1.0 / np.sqrt(128.0))

ENGS = ("sync", "gpsimd", "tensor", "scalar", "vector")


class Sem:
    __slots__ = ("h", "n", "name")

    def __init__(self, nc, name):
        self.h = nc.alloc_semaphore(name)
        self.n = 0
        self.name = name


class Emit:
    def __init__(self):
        self.steps = {e: [] for e in ENGS}
        self.hwm = {e: {} for e in ENGS}

    def step(self, eng, fn, waits=(), sem=None, dma=False, force=False):
        ws = []
        for w in waits:
            if w is None:
                continue
            s, v = w
            if force or self.hwm[eng].get(s, -1) < v:
                if self.hwm[eng].get(s, -1) < v:
                    self.hwm[eng][s] = v
                ws.append((s, v))
        tokn = None
        amt = 0
        if sem is not None:
            amt = 16 if dma else 1
            sem.n += amt
            tokn = (sem, sem.n)
        self.steps[eng].append((fn, ws, sem, amt))
        return tokn


DBG_NTC = NTC
DBG_AQ = 2
DBG_BH = NH


def build_program(dump=None):
    nc = bass.Bass("TRN2", target_bir_lowering=False, debug=False)

    d_xT = nc.dram_tensor("xT", [D, T], F32, kind="ExternalInput").ap()
    d_xTown = nc.dram_tensor("xTown", [D, TOK], F32, kind="ExternalInput").ap()
    d_cosT = nc.dram_tensor("cosT", [HD, T], F32, kind="ExternalInput").ap()
    d_sinT = nc.dram_tensor("sinT", [HD, T], F32, kind="ExternalInput").ap()
    d_cosO = nc.dram_tensor("cosO", [HD, TOK], F32, kind="ExternalInput").ap()
    d_sinO = nc.dram_tensor("sinO", [HD, TOK], F32, kind="ExternalInput").ap()
    d_waTkv = nc.dram_tensor("waTkv", [128, DC * 2 * 128], F32, kind="ExternalInput").ap()
    d_waTq = nc.dram_tensor("waTq", [128, DC * NH * 128], F32, kind="ExternalInput").ap()
    d_woTt = nc.dram_tensor("woTt", [DC, 128, D], F32, kind="ExternalInput").ap()
    d_wa = nc.dram_tensor("wa", [128, DC * NE], F32, kind="ExternalInput").ap()
    d_g = nc.dram_tensor("g_t", [NEX, FC, 128, DC * 128], F32, kind="ExternalInput").ap()
    d_w1 = nc.dram_tensor("w1_t", [NEX, FC, 128, DC * 128], F32, kind="ExternalInput").ap()
    d_w2 = nc.dram_tensor("w2p_t", [NEX // 2, DC, 128, 2048], F32, kind="ExternalInput").ap()
    d_gb = nc.dram_tensor("gb_t", [128, NEX * FC], F32, kind="ExternalInput").ap()
    d_b1 = nc.dram_tensor("b1_t", [128, NEX * FC], F32, kind="ExternalInput").ap()
    d_rb2 = nc.dram_tensor("rb2sb2", [34, D], F32, kind="ExternalInput").ap()
    d_idn = nc.dram_tensor("idn", [128, 128], F32, kind="ExternalInput").ap()
    d_y = nc.dram_tensor("y", [D, TOK], F32, kind="ExternalOutput").ap()
    d_x1s = nc.dram_tensor("x1s", [D, TOK], F32, kind="Internal").ap()
    y_v = d_y.rearrange("(c p) t -> p c t", p=128)

    arA = nc.alloc_sbuf_tensor("arA", [128, 32768], BF16)
    arB = nc.alloc_sbuf_tensor("arB", [128, 32768], BF16)
    arC = nc.alloc_sbuf_tensor("arC", [128, 8192], BF16)
    arD = nc.alloc_sbuf_tensor("arD", [128, 8192], BF16)
    arS = nc.alloc_sbuf_tensor("arS", [128, 16384], BF16)

    ones_f = nc.alloc_sbuf_tensor("ones_f", [1, 128], F32)
    ones_fc = nc.alloc_sbuf_tensor("ones_fc", [128, 1], F32)
    ones_b = nc.alloc_sbuf_tensor("ones_b", [128, 1], BF16)
    ones2f = nc.alloc_sbuf_tensor("ones2f", [2, 512], F32)
    idn_f = nc.alloc_sbuf_tensor("idn_f", [128, 128], F32)
    idn_b = nc.alloc_sbuf_tensor("idn_b", [128, 128], BF16)
    eps_t = nc.alloc_sbuf_tensor("eps_t", [1, 1], F32)
    zero1 = nc.alloc_sbuf_tensor("zero1", [1, 1], F32)
    n1e9 = nc.alloc_sbuf_tensor("n1e9", [128, 1], F32)
    gb_sb = nc.alloc_sbuf_tensor("gb_sb", [128, NEX * FC], F32)
    b1_sb = nc.alloc_sbuf_tensor("b1_sb", [128, NEX * FC], F32)
    wa_sb = nc.alloc_sbuf_tensor("wa_sb", [128, DC * NE], BF16)
    biasM = nc.alloc_sbuf_tensor("biasM", [128, 1], F32)
    c_fm = nc.alloc_sbuf_tensor("c_fm", [16, TOK], F32)
    rb2_sb = nc.alloc_sbuf_tensor("rb2_sb", [16, 128], F32)
    sb2_sb = nc.alloc_sbuf_tensor("sb2_sb", [2, 128], F32)
    kmax = nc.alloc_sbuf_tensor("kmax", [1, 1], F32)
    qmax = nc.alloc_sbuf_tensor("qmax", [1, 1], F32)
    sc1 = nc.alloc_sbuf_tensor("sc1", [1, 1], F32)
    sc2 = nc.alloc_sbuf_tensor("sc2", [1, 1], F32)

    def carve(arena, c0, c1, dt=None):
        v = arena[:, c0:c1]
        if dt is F32:
            v = v.bitcast(F32)
        return v

    pq01 = nc.alloc_psum_tensor("pq01", [128, 1024], F32)
    pq23 = nc.alloc_psum_tensor("pq23", [128, 1024], F32)
    pq45 = nc.alloc_psum_tensor("pq45", [128, 1024], F32)
    ps6 = nc.alloc_psum_tensor("ps6", [128, 512], F32)
    ps7 = nc.alloc_psum_tensor("ps7", [128, 512], F32)
    bank = [ps6[:, :], ps7[:, :], pq45[:, 0:512], pq45[:, 512:1024],
            pq01[:, 0:512], pq01[:, 512:1024],
            pq23[:, 0:512], pq23[:, 512:1024]]

    sSt = Sem(nc, "sSt")
    sT = Sem(nc, "sT")
    sA = Sem(nc, "sA")
    sV = Sem(nc, "sV")
    sCst = Sem(nc, "sCst")
    sCsS = Sem(nc, "sCsS")
    sWq = Sem(nc, "sWq")
    sCos = Sem(nc, "sCos")
    sSin = Sem(nc, "sSin")
    sXt = Sem(nc, "sXt")
    sRb2 = Sem(nc, "sRb2")
    sCrow = Sem(nc, "sCrow")
    sXb = [Sem(nc, "sXb0"), Sem(nc, "sXb1")]
    sWo = [Sem(nc, "sWo0"), Sem(nc, "sWo1")]
    sG = [Sem(nc, "sG0"), Sem(nc, "sG1")]
    sW1 = [Sem(nc, "sW10"), Sem(nc, "sW11")]
    sW2 = [Sem(nc, "sW20"), Sem(nc, "sW21")]
    sX1f = [Sem(nc, "sX1f0"), Sem(nc, "sX1f1")]
    sEp = Sem(nc, "sEp")
    sDmp = Sem(nc, "sDmp")
    ALL_DMA_SEMS = ([sSt, sDmp, sCst, sCsS, sWq, sCos, sSin, sXt, sRb2, sCrow]
                    + sXb + sWo + sG + sW1 + sW2 + sX1f)
    EM = Emit()

    def E_(eng, fn, waits=(), sem=None, dma=False, force=False):
        return EM.step(eng, fn, waits, sem, dma, force)

    E_("sync", lambda e: e.dma_start(out=gb_sb[:], in_=d_gb[:, :]), sem=sCsS, dma=True)
    E_("sync", lambda e: e.dma_start(out=b1_sb[:], in_=d_b1[:, :]), sem=sCsS, dma=True)
    E_("sync", lambda e: e.dma_start(out=idn_f[:], in_=d_idn[:, :]), sem=sCsS, dma=True)
    E_("gpsimd", lambda e: e.dma_start(out=idn_b[:], in_=d_idn[:, :]), sem=sCst, dma=True)
    E_("gpsimd", lambda e: e.dma_start(out=wa_sb[:], in_=d_wa[:, :]), sem=sCst, dma=True)
    E_("gpsimd", lambda e: e.dma_start(out=carve(arS, 0, 4096), in_=d_waTkv[:, :]), sem=sCst, dma=True)
    t_gb = t_b1c = t_idf = (sCsS, sCsS.n)
    t_idb = t_wa = t_wkv = (sCst, sCst.n)
    t_onf = E_("vector", lambda e: e.memset(ones_f[:], 1.0), sem=sV)
    t_onfc = E_("vector", lambda e: e.memset(ones_fc[:], 1.0), sem=sV)
    t_onb = E_("vector", lambda e: e.memset(ones_b[:], 1.0), sem=sV)
    t_on2 = E_("vector", lambda e: e.memset(ones2f[:], 1.0), sem=sV)
    t_eps = E_("vector", lambda e: e.memset(eps_t[:], EPS), sem=sV)
    t_z1 = E_("vector", lambda e: e.memset(zero1[:], 0.0), sem=sV)
    t_n9 = E_("vector", lambda e: e.memset(n1e9[:], -1.0e9), sem=sV)

    S_t1 = carve(arS, 4096, 5120, F32)
    S_t2 = carve(arS, 5120, 6144, F32)
    S_rb = carve(arS, 6144, 7168, F32)
    S_cosr = carve(arS, 7168, 8192, F32)
    S_sinr = carve(arS, 8192, 9216, F32)
    S_vtmp = carve(arS, 9216, 9728)
    S_xsq = [carve(arS, 9728 + 512 * i, 9728 + 512 * (i + 1)) for i in range(2)]
    S_R3 = arS[0:1, 10752:11776].bitcast(F32)
    S_cosc = carve(arS, 11776, 12800, F32)
    S_sinc = carve(arS, 12800, 13824, F32)
    S_wsum = arS[0:1, 13824:14336].bitcast(F32)
    S_R1 = arS[0:1, 14336:15360].bitcast(F32)
    S_R2 = arS[0:1, 15360:16384].bitcast(F32)
    Q_wsum = arS[0:1, 0:4096].bitcast(F32)

    xb_bufs = [carve(arB, 0, 8192), carve(arB, 8192, 16384)]
    waTq_sb = carve(arB, 16384, 32768)
    K_fm = arC[:, :]
    V_tm = arD[:, :]
    Q_sb = arA[:, :]

    ST = {
        "stats_war": None, "sq_war": {}, "rb_war": None, "cos_war": None,
        "qkv_war": {}, "qkv_rot": 0, "v_war": None, "tr_war": None,
    }

    def a_chunk(xb, x_dram_view, wsum_row, dout_list, lhsT_for, evac,
                tok_prev_use, cosd, sind, xbsem):
        pS, pQ = bank[2], bank[3]
        t_ld = E_("gpsimd", lambda e, o=xb, i=x_dram_view: e.dma_start(
            out=o.rearrange("p (c t) -> p c t", c=16), in_=i),
            waits=[tok_prev_use], sem=xbsem, dma=True)
        t_cos = E_("sync", lambda e, i=cosd: e.dma_start(out=S_cosc, in_=i),
                   waits=[ST["cos_war"]], sem=sCos, dma=True)
        t_sin = E_("sync", lambda e, i=sind: e.dma_start(out=S_sinc, in_=i),
                   waits=[ST["cos_war"]], sem=sSin, dma=True)
        tok_sum = None
        for dinc in range(DC):
            tok_sum = E_("tensor", lambda e, d=dinc, x=xb: e.matmul(
                pS[0:1, :], ones_b[:, :], x[:, d * 512:(d + 1) * 512],
                start=(d == 0), stop=(d == DC - 1)),
                waits=[t_ld, ST["stats_war"], t_onb],
                sem=(sT if dinc == DC - 1 else None))
        for dinc in range(DC):
            t_sq = E_("vector", lambda e, d=dinc, x=xb: e.tensor_tensor(
                out=S_xsq[d % 2][:], in0=x[:, d * 512:(d + 1) * 512],
                in1=x[:, d * 512:(d + 1) * 512], op=ALU.mult),
                waits=[t_ld, ST["sq_war"].get(dinc % 2)], sem=sV)
            ST["sq_war"][dinc % 2] = E_("tensor", lambda e, d=dinc: e.matmul(
                pQ[0:1, :], ones_b[:, :], S_xsq[d % 2][:],
                start=(d == 0), stop=(d == DC - 1)),
                waits=[t_sq, ST["stats_war"]], sem=sT)
        tok_sq = ST["sq_war"][(DC - 1) % 2]
        pcT, pcV = ST.get("pcT"), ST.get("pcV")
        t_mean = E_("scalar", lambda e: e.mul(S_R1[:, :], pS[0:1, :], 1.0 / 2048.0),
                    waits=[tok_sum, pcT, pcV], sem=sA)
        t_msq = E_("scalar", lambda e: e.mul(S_R2[:, :], pQ[0:1, :], 1.0 / 2048.0),
                   waits=[tok_sq, pcT, pcV], sem=sA)
        ST["stats_war"] = t_msq
        t_m2 = E_("vector", lambda e: e.tensor_tensor(out=S_R3[:, :], in0=S_R1[:, :],
                  in1=S_R1[:, :], op=ALU.mult), waits=[t_mean, pcV], sem=sV)
        t_var = E_("vector", lambda e: e.tensor_tensor(out=S_R2[:, :], in0=S_R2[:, :],
                   in1=S_R3[:, :], op=ALU.subtract), waits=[t_msq, t_m2], sem=sV)
        t_varc = E_("vector", lambda e: e.tensor_scalar(out=S_R2[:, :], in0=S_R2[:, :],
                    scalar1=zero1[0:1, 0:1], scalar2=None, op0=ALU.max),
                    waits=[t_var, t_z1], sem=sV)
        t_ln = E_("scalar", lambda e: e.activation(out=S_R2[:, :], in_=S_R2[:, :],
                  func=AF.Ln, bias=eps_t[0:1, 0:1]), waits=[t_varc, t_eps], sem=sA)
        t_r = E_("scalar", lambda e: e.activation(out=S_R2[:, :], in_=S_R2[:, :],
                 func=AF.Exp, scale=-0.5), waits=[t_ln], sem=sA)
        t_mr = E_("vector", lambda e: e.tensor_tensor(out=S_R1[:, :], in0=S_R1[:, :],
                  in1=S_R2[:, :], op=ALU.mult), waits=[t_r], sem=sV)
        t_rbm = E_("tensor", lambda e: e.matmul(bank[6][:, :], ones_f[:, :],
                   S_R2[0:1, :], start=True, stop=True),
                   waits=[t_r, t_onf, ST["rb_war"]], sem=sT)
        t_rb = E_("vector", lambda e: e.tensor_copy(out=S_rb[:], in_=bank[6][:, :]),
                  waits=[t_rbm, pcV], sem=sV)
        ST["rb_war"] = t_rb
        t_cr = E_("vector", lambda e: e.tensor_tensor(out=S_cosr[:], in0=S_cosc[:],
                  in1=S_rb[:], op=ALU.mult), waits=[t_cos, t_rb, pcV], sem=sV)
        t_sr = E_("vector", lambda e: e.tensor_tensor(out=S_sinr[:], in0=S_sinc[:],
                  in1=S_rb[:], op=ALU.mult), waits=[t_sin, t_rb, pcV], sem=sV)
        ST["cos_war"] = t_sr
        for j in dout_list:
            pb = bank[ST["qkv_rot"] % 2]
            key = ST["qkv_rot"] % 2
            ST["qkv_rot"] += 1
            war = ST["qkv_war"].get(key)
            for dinc in range(DC):
                E_("tensor", lambda e, d=dinc, jj=j, p=pb, x=xb: e.matmul(
                    p[:, :], lhsT_for(jj, d), x[:, d * 512:(d + 1) * 512],
                    start=(d == 0), stop=False),
                    waits=[t_ld, war] if dinc == 0 else [])
            tok_mm = E_("tensor", lambda e, jj=j, p=pb: e.matmul(
                p[:, :], wsum_row[0:1, jj * 128:(jj + 1) * 128], S_R1[0:1, :],
                start=False, stop=True), waits=[t_mr], sem=sT)
            ST["qkv_war"][key] = evac(j, pb, tok_mm, t_cr, t_sr, t_rb)
        ST["pcT"] = (sT, sT.n)
        ST["pcV"] = (sV, sV.n)
        return t_ld

    def rope_evac(dst, psO, tok_mm, t_cr, t_sr):
        war = ST.get("rope_war")
        t_a = E_("vector", lambda e: e.tensor_tensor(out=S_t1[:], in0=psO[:, :],
                 in1=S_cosr[:], op=ALU.mult), waits=[tok_mm, t_cr, war], sem=sV)
        t_b = E_("vector", lambda e: e.tensor_tensor(out=S_t2[0:64, :], in0=psO[64:128, :],
                 in1=S_sinr[0:64, :], op=ALU.mult), waits=[tok_mm, t_sr, war], sem=sV)
        t_c = E_("vector", lambda e: e.tensor_tensor(out=S_t2[64:128, :], in0=psO[0:64, :],
                 in1=S_sinr[64:128, :], op=ALU.mult), waits=[tok_mm, t_sr, war], sem=sV)
        E_("vector", lambda e, d=dst: e.tensor_tensor(out=d[0:64, :], in0=S_t1[0:64, :],
           in1=S_t2[0:64, :], op=ALU.subtract), waits=[t_a, t_b], sem=sV)
        t_e = E_("vector", lambda e, d=dst: e.tensor_tensor(out=d[64:128, :],
                 in0=S_t1[64:128, :], in1=S_t2[64:128, :], op=ALU.add),
                 waits=[t_a, t_c], sem=sV)
        ST["rope_war"] = t_e
        return t_e

    wkv = carve(arS, 0, 4096)
    tok_wm = None
    for kv in range(2):
        for dinc in range(DC):
            tok_wm = E_("tensor", lambda e, d=dinc, k=kv: e.matmul(
                bank[4][0:1, k * 128:(k + 1) * 128], ones_b[:, :],
                wkv[:, (d * 2 + k) * 128:(d * 2 + k + 1) * 128],
                start=(d == 0), stop=(d == DC - 1)),
                waits=[t_wkv, t_onb], sem=(sT if dinc == DC - 1 else None))
    t_wsum = E_("vector", lambda e: e.tensor_scalar(
        out=S_wsum[:, :], in0=bank[4][0:1, 0:256], scalar1=-1.0,
        scalar2=None, op0=ALU.mult), waits=[tok_wm], sem=sV)
    ST["tr_war"] = t_wsum

    tok_xb_war = [None, None]
    for tc in range(DBG_NTC):
        sl = slice(tc * 512, (tc + 1) * 512)
        xb = xb_bufs[tc % 2]
        xdv = d_xT.rearrange("(c p) t -> p c t", p=128)[:, :, sl]

        def evac_kv(j, psO, tok_mm, t_cr, t_sr, t_rb, tc=tc):
            if j == 0:
                return rope_evac(K_fm[:, tc * 512:(tc + 1) * 512], psO, tok_mm, t_cr, t_sr)
            t_v = E_("vector", lambda e: e.tensor_tensor(
                out=S_vtmp[:], in0=psO[:, :], in1=S_rb[:], op=ALU.mult),
                waits=[tok_mm, t_rb, ST["v_war"]], sem=sV)
            ptr = bank[4].bitcast(BF16)
            ttr = None
            for i in range(4):
                ttr = E_("tensor", lambda e, ii=i: e.transpose(
                    ptr[:, ii * 128:(ii + 1) * 128],
                    S_vtmp[:, ii * 128:(ii + 1) * 128], idn_b[:, :]),
                    waits=[t_v, t_idb, ST["tr_war"]], sem=(sT if i == 3 else None))
            t_vc = E_("vector", lambda e: e.tensor_copy(
                out=V_tm[:, tc * 512:(tc + 1) * 512], in_=ptr[:, 0:512]),
                waits=[ttr], sem=sV)
            ST["v_war"] = t_vc
            ST["tr_war"] = t_vc
            return t_vc

        def lhsT_kv(j, dinc):
            return wkv[:, (dinc * 2 + j) * 128:(dinc * 2 + j + 1) * 128]

        a_chunk(xb, xdv, S_wsum, [0, 1], lhsT_kv, evac_kv,
                tok_xb_war[tc % 2], d_cosT[:, sl], d_sinT[:, sl], sXb[tc % 2])
        tok_xb_war[tc % 2] = (sT, sT.n)
    tok_akv_T = (sT, sT.n)

    for qh in range(DBG_AQ):
        t_wq = E_("gpsimd", lambda e, q0=qh: e.dma_start(
            out=waTq_sb, in_=d_waTq[:, q0 * 16384:(q0 + 1) * 16384]),
            waits=[(sT, sT.n)], sem=sWq, dma=True)
        prev_ts = (sV, sV.n)
        for jl in range(8):
            tok_wm = None
            for dinc in range(DC):
                tok_wm = E_("tensor", lambda e, d=dinc, j=jl: e.matmul(
                    bank[4][0:1, 0:128], ones_b[:, :],
                    waTq_sb[:, (j * DC + d) * 128:(j * DC + d + 1) * 128],
                    start=(d == 0), stop=(d == DC - 1)),
                    waits=[t_wq, prev_ts if dinc == 0 else None,
                           ST["tr_war"] if jl == 0 else None],
                    sem=(sT if dinc == DC - 1 else None))
            prev_ts = E_("vector", lambda e, j=jl: e.tensor_scalar(
                out=Q_wsum[:, j * 128:(j + 1) * 128], in0=bank[4][0:1, 0:128],
                scalar1=-1.0, scalar2=None, op0=ALU.mult),
                waits=[tok_wm, tok_akv_T], sem=sV)
        tok_qw_v = (sV, sV.n)

        for tq in range(NQC):
            sl = slice(tq * 512, (tq + 1) * 512)
            xb = xb_bufs[tq % 2]
            xdv = d_xTown.rearrange("(c p) t -> p c t", p=128)[:, :, sl]

            def evac_q(j, psO, tok_mm, t_cr, t_sr, t_rb, tq=tq, qh=qh):
                hg = qh * 8 + j
                dst = Q_sb[:, hg * TOK + tq * 512: hg * TOK + (tq + 1) * 512]
                return rope_evac(dst, psO, tok_mm, t_cr, t_sr)

            def lhsT_q(j, dinc):
                return waTq_sb[:, (j * DC + dinc) * 128:(j * DC + dinc + 1) * 128]

            if tq == 0:
                for key in (0, 1):
                    prev = ST["qkv_war"].get(key)
                    if prev is None or (prev[0] is tok_qw_v[0] and prev[1] < tok_qw_v[1]):
                        ST["qkv_war"][key] = tok_qw_v
            a_chunk(xb, xdv, Q_wsum, list(range(8)), lhsT_q, evac_q,
                    tok_xb_war[tq % 2], d_cosO[:, sl], d_sinO[:, sl], sXb[tq % 2])
            tok_xb_war[tq % 2] = (sT, sT.n)
    tok_aq_T = (sT, sT.n)
    tok_aq_V = (sV, sV.n)

    if dump == "A":
        _dump(EM, y_v, [(K_fm, 0, 8192), (V_tm, 8192, 8192),
                        (Q_sb[:, 0:4096], 16384, 4096),
                        (Q_sb[:, 15 * TOK:16 * TOK], 20480, 2048)],
              [tok_aq_T, tok_aq_V], sSt, sDmp,
              fitems=[(S_rb, 12), (S_cosr, 13), (S_sinr, 14), (S_cosc, 15)])
        _tail(EM, ALL_DMA_SEMS)
        return nc, EM

    ksq = carve(arS, 0, 8192)
    t_ksq = E_("vector", lambda e: e.tensor_tensor(out=ksq[:], in0=K_fm[:], in1=K_fm[:],
               op=ALU.mult), waits=[tok_aq_T, tok_aq_V], sem=sV)
    E_("vector", lambda e: e.memset(kmax[:], 0.0), sem=sV)
    E_("vector", lambda e: e.memset(qmax[:], 0.0), sem=sV)
    t_mx0 = (sV, sV.n)
    prev_red = t_mx0
    for j in range(16):
        pm = bank[j % 2]
        t_mm = E_("tensor", lambda e, jj=j, p=pm: e.matmul(
            p[0:1, :], ones_b[:, :], ksq[:, jj * 512:(jj + 1) * 512],
            start=True, stop=True), waits=[t_ksq, prev_red], sem=sT)
        t_red = E_("vector", lambda e, p=pm: e.tensor_reduce(
            out=sc1[:, :], in_=p[0:1, :], axis=AX.X, op=ALU.max),
            waits=[t_mm, prev_red], sem=sV)
        prev_red = E_("vector", lambda e: e.tensor_tensor(
            out=kmax[:, :], in0=kmax[:, :], in1=sc1[:, :], op=ALU.max),
            waits=[t_red], sem=sV)
    for piece in range(4):
        t_qsq = E_("vector", lambda e, pp=piece: e.tensor_tensor(
            out=ksq[:], in0=Q_sb[:, pp * 8192:(pp + 1) * 8192],
            in1=Q_sb[:, pp * 8192:(pp + 1) * 8192], op=ALU.mult),
            waits=[prev_red, (sT, sT.n)], sem=sV)
        for j in range(16):
            pm = bank[j % 2]
            t_mm = E_("tensor", lambda e, jj=j, p=pm: e.matmul(
                p[0:1, :], ones_b[:, :], ksq[:, jj * 512:(jj + 1) * 512],
                start=True, stop=True), waits=[t_qsq, prev_red], sem=sT)
            t_red = E_("vector", lambda e, p=pm: e.tensor_reduce(
                out=sc1[:, :], in_=p[0:1, :], axis=AX.X, op=ALU.max),
                waits=[t_mm, prev_red], sem=sV)
            prev_red = E_("vector", lambda e: e.tensor_tensor(
                out=qmax[:, :], in0=qmax[:, :], in1=sc1[:, :], op=ALU.max),
                waits=[t_red], sem=sV)
    t_l1 = E_("scalar", lambda e: e.activation(out=sc1[:, :], in_=kmax[:, :], func=AF.Ln),
              waits=[prev_red], sem=sA)
    t_l2 = E_("scalar", lambda e: e.activation(out=sc2[:, :], in_=qmax[:, :], func=AF.Ln),
              waits=[prev_red], sem=sA)
    t_ad = E_("vector", lambda e: e.tensor_tensor(out=sc1[:, :], in0=sc1[:, :],
              in1=sc2[:, :], op=ALU.add), waits=[t_l1, t_l2], sem=sV)
    t_ex = E_("scalar", lambda e: e.activation(out=sc1[:, :], in_=sc1[:, :],
              func=AF.Exp, scale=0.5), waits=[t_ad], sem=sA)
    t_ng = E_("vector", lambda e: e.tensor_scalar(out=sc1[:, :], in0=sc1[:, :],
              scalar1=-ISQ, scalar2=None, op0=ALU.mult), waits=[t_ex], sem=sV)
    t_bm = E_("tensor", lambda e: e.matmul(bank[2][:, 0:1], ones_f[:, :], sc1[0:1, 0:1],
              start=True, stop=True), waits=[t_ng, t_onf], sem=sT)
    t_bM = E_("vector", lambda e: e.tensor_copy(out=biasM[:, :], in_=bank[2][:, 0:1]),
              waits=[t_bm], sem=sV)

    out_fm = arB[:, :]
    EB2 = [carve(arS, i * 1024, (i + 1) * 1024) for i in range(6)]
    B_rb = carve(arS, 6144, 7168, F32)
    B_acc = carve(arS, 7168, 8192, F32)
    FT = [carve(arS, 8192 + 512 * i, 8192 + 512 * (i + 1)) for i in range(3)]
    rcrow = arS[0:1, 9728:10752].bitcast(F32)
    PP = [pq01[:, :], pq23[:, :], pq45[:, :]]
    pavb = bank[0]
    psmb = bank[1]
    NPAIR = KCN // 2

    prev_last_av = t_bM
    prev_last_fold = t_bM
    prev_last_exp = t_bM
    prev_accread = t_bM
    prev_rb_read = t_bM
    prev_psm_read = t_bM
    hq = 0
    for h in range(DBG_BH):
        for q in range(NQC):
            qsl = slice(h * TOK + q * 512, h * TOK + (q + 1) * 512)
            base_war = (sEp, hq) if hq >= 1 else None
            exp_toks = {}
            av_toks = {}
            fold_toks = {}
            facc = None
            for kp in range(NPAIR + 2):
                if kp < NPAIR:
                    w_sc = [exp_toks.get(kp - 3)]
                    if kp == 0:
                        w_sc += [tok_aq_V, t_bM, prev_last_av, prev_last_fold]
                    if kp < 3:
                        w_sc.append(prev_last_exp)
                    E_("tensor", lambda e, k=kp, hs=qsl: e.matmul(
                        PP[k % 3][:, 0:512], K_fm[:, 2 * k * 128:(2 * k + 1) * 128],
                        Q_sb[:, hs], start=True, stop=True), waits=w_sc)
                    E_("tensor", lambda e, k=kp, hs=qsl: e.matmul(
                        PP[k % 3][:, 512:1024], K_fm[:, (2 * k + 1) * 128:(2 * k + 2) * 128],
                        Q_sb[:, hs], start=True, stop=True), sem=sT)
                    sc_tok = (sT, sT.n)
                    w_ex = [sc_tok, av_toks.get(kp - 6), fold_toks.get(kp - 6), t_bM]
                    if kp < 6:
                        w_ex += [prev_last_av, prev_last_fold]
                    exp_toks[kp] = E_("scalar", lambda e, k=kp: e.activation(
                        out=EB2[k % 6][:], in_=PP[k % 3][:, :], func=AF.Exp,
                        scale=ISQ, bias=biasM[:, 0:1]), waits=w_ex, sem=sA)
                j = kp - 2
                if 0 <= j < NPAIR:
                    E_("tensor", lambda e, k=j: e.matmul(
                        pavb, V_tm[:, 2 * k * 128:(2 * k + 1) * 128],
                        EB2[k % 6][:, 0:512],
                        start=(k == 0), stop=False, skip_group_check=True),
                        waits=[exp_toks[j]] + ([base_war] if j == 0 else []),
                        force=(j == 0))
                    av_toks[j] = E_("tensor", lambda e, k=j: e.matmul(
                        pavb, V_tm[:, (2 * k + 1) * 128:(2 * k + 2) * 128],
                        EB2[k % 6][:, 512:1024],
                        start=False, stop=(k == NPAIR - 1), skip_group_check=True),
                        sem=sT)
                    if j % 2 == 1:
                        t_f0 = E_("vector", lambda e, k=j - 1: e.tensor_tensor(
                            out=FT[0][:], in0=EB2[k % 6][:, 0:512],
                            in1=EB2[k % 6][:, 512:1024], op=ALU.add),
                            waits=[exp_toks[j - 1], facc], sem=sV)
                        t_f1 = E_("vector", lambda e, k=j: e.tensor_tensor(
                            out=FT[1][:], in0=EB2[k % 6][:, 0:512],
                            in1=EB2[k % 6][:, 512:1024], op=ALU.add),
                            waits=[exp_toks[j], facc], sem=sV)
                        t_f2 = E_("vector", lambda e: e.tensor_tensor(
                            out=FT[2][:], in0=FT[0][:], in1=FT[1][:], op=ALU.add),
                            waits=[t_f0, t_f1], sem=sV)
                        if j == 1:
                            facc = E_("vector", lambda e: e.tensor_copy(
                                out=B_acc[:], in_=FT[2][:]),
                                waits=[t_f2, prev_accread], sem=sV)
                        else:
                            facc = E_("vector", lambda e: e.tensor_tensor(
                                out=B_acc[:], in0=B_acc[:], in1=FT[2][:], op=ALU.add),
                                waits=[t_f2], sem=sV)
                        fold_toks[j - 1] = facc
                        fold_toks[j] = facc
            t_sum = E_("tensor", lambda e: e.matmul(
                psmb[0:1, :], ones_fc[:, :], B_acc[:], start=True, stop=True),
                waits=[facc, prev_psm_read, t_onfc], sem=sT)
            prev_accread = t_sum
            t_rc = E_("vector", lambda e: e.reciprocal(
                out=rcrow[:, :], in_=psmb[0:1, :]), waits=[t_sum], sem=sV)
            t_bc = E_("tensor", lambda e: e.matmul(
                psmb[:, :], ones_f[:, :], rcrow[0:1, :], start=True, stop=True),
                waits=[t_rc], sem=sT)
            t_rb2 = E_("vector", lambda e: e.tensor_copy(
                out=B_rb[:], in_=psmb[:, :]), waits=[t_bc, prev_rb_read], sem=sV)
            tok_epilog = E_("vector", lambda e, hs=qsl: e.tensor_tensor(
                out=out_fm[:, hs], in0=pavb, in1=B_rb[:], op=ALU.mult),
                waits=[t_rb2, av_toks[NPAIR - 1]], sem=sEp)
            tok_epilog = (sEp, hq + 1)
            prev_rb_read = tok_epilog
            prev_psm_read = tok_epilog
            prev_last_av = av_toks[NPAIR - 1]
            prev_last_fold = facc
            prev_last_exp = exp_toks[NPAIR - 1]
            hq += 1
    tok_b_done = tok_epilog

    if dump == "B":
        _dump(EM, y_v, [(out_fm, 0, 32768)], [tok_b_done, (sT, sT.n)], sSt, sDmp)
        _tail(EM, ALL_DMA_SEMS)
        return nc, EM

    woT = [carve(arS, 0, 2048), carve(arS, 2048, 4096)]
    r2b = carve(arS, 4096, 5120, F32)
    m2b = carve(arS, 5120, 6144, F32)
    x1b_t = carve(arS, 6144, 6656)
    x1sq_t = carve(arS, 6656, 7168)
    xt_f = carve(arS, 7168, 8192, F32)
    aff_f = arS[0:16, 8192:9216].bitcast(F32)
    afftm = arS[:, 9216:9248].bitcast(F32)
    tk_m1 = arS[:, 9248:9250].bitcast(F32)
    tk_m2 = arS[:, 9250:9252].bitcast(F32)
    tk_eq1 = arS[:, 9252:9284].bitcast(F32)
    tk_eq2 = arS[:, 9284:9316].bitcast(F32)
    tk_a2 = arS[:, 9316:9348].bitcast(F32)
    tk_d = arS[:, 9348:9350].bitcast(F32)
    tk_t1 = arS[:, 9350:9382].bitcast(F32)
    tk_ctm = arS[:, 9382:9398]
    C_tmp = carve(arS, 13312, 14336, F32)
    C_R1 = arS[0:1, 10240:11264].bitcast(F32)
    C_R2 = arS[0:1, 11264:12288].bitcast(F32)
    C_R3 = arS[0:1, 12288:13312].bitcast(F32)
    x1v = [arC[:, :].bitcast(F32), arD[:, :].bitcast(F32)]

    pC = [bank[0], bank[1]]
    pS2, pQ2 = bank[2], bank[3]
    pR2 = bank[4]
    pAf = bank[5]
    pTr = bank[6]
    h2 = arA[:, :]

    tok_bT = (sT, sT.n)
    tok_woT_war = [[tok_b_done, tok_bT], [tok_b_done, tok_bT]]
    tok_x1_war = tok_bT
    tok_cd_last = None
    tok_d_apply = []
    for q in range(NQC):
        sum_tok = None
        sq_tok = None
        x1toks = {}
        st_toks = {}
        for dc in range(DC):
            idx = dc % 2
            wol = tok_woT_war[idx]
            t_wo = E_("gpsimd", lambda e, d=dc, i=idx: e.dma_start(
                out=woT[i], in_=d_woTt[d]),
                waits=(wol if isinstance(wol, list) else [wol]),
                sem=sWo[idx], dma=True)
            pb = pC[dc % 2]
            tok_mm = None
            for chc in range(NH):
                tok_mm = E_("tensor", lambda e, c=chc, p=pb, i=idx, qq=q: e.matmul(
                    p[:, :], woT[i][:, c * 128:(c + 1) * 128],
                    out_fm[:, c * TOK + qq * 512:c * TOK + (qq + 1) * 512],
                    start=(c == 0), stop=(c == NH - 1)),
                    waits=[t_wo, tok_b_done, x1toks.get(dc - 2)] if chc == 0 else [],
                    sem=(sT if chc == NH - 1 else None))
            tok_woT_war[idx] = tok_mm
            t_xt = E_("sync", lambda e, d=dc, qq=q: e.dma_start(
                out=xt_f, in_=d_xTown.rearrange("(c p) t -> p c t", p=128)
                [:, d, qq * 512:(qq + 1) * 512]), waits=[tok_x1_war], sem=sXt, dma=True)
            x1sl = x1v[dc // 8][:, (dc % 8) * 512:(dc % 8 + 1) * 512]
            t_x1 = E_("vector", lambda e, s=x1sl, p=pb: e.tensor_tensor(
                out=s, in0=p[:, :], in1=xt_f[:], op=ALU.add),
                waits=[tok_mm, t_xt, (sSt, sSt.n)], sem=sV)
            x1toks[dc] = t_x1
            tok_x1_war = t_x1
            st_toks[dc] = E_("sync", lambda e, s=x1sl, d=dc, qq=q: e.dma_start(
                out=d_x1s[d * 128:(d + 1) * 128, qq * 512:(qq + 1) * 512], in_=s),
                waits=[t_x1], sem=sSt, dma=True)
            t_xb = E_("vector", lambda e, s=x1sl: e.tensor_copy(out=x1b_t[:], in_=s),
                      waits=[t_x1, sq_tok], sem=sV)
            t_xq = E_("vector", lambda e: e.tensor_tensor(out=x1sq_t[:], in0=x1b_t[:],
                      in1=x1b_t[:], op=ALU.mult), waits=[t_xb], sem=sV)
            sum_tok = E_("tensor", lambda e, d=dc: e.matmul(
                pS2[0:1, :], ones_b[:, :], x1b_t[:],
                start=(d == 0), stop=(d == DC - 1)), waits=[t_xb], sem=sT)
            sq_tok = E_("tensor", lambda e, d=dc: e.matmul(
                pQ2[0:1, :], ones_b[:, :], x1sq_t[:],
                start=(d == 0), stop=(d == DC - 1)), waits=[t_xq], sem=sT)
        cdT, cdV = (sT, sT.n), (sV, sV.n)
        t_mean = E_("scalar", lambda e: e.mul(C_R1[:, :], pS2[0:1, :], 1.0 / 2048.0),
                    waits=[sum_tok, cdT, cdV], sem=sA)
        t_msq = E_("scalar", lambda e: e.mul(C_R2[:, :], pQ2[0:1, :], 1.0 / 2048.0),
                   waits=[sq_tok, cdT, cdV], sem=sA)
        t_m2 = E_("vector", lambda e: e.tensor_tensor(out=C_R3[:, :], in0=C_R1[:, :],
                  in1=C_R1[:, :], op=ALU.mult), waits=[t_mean], sem=sV)
        t_var = E_("vector", lambda e: e.tensor_tensor(out=C_R2[:, :], in0=C_R2[:, :],
                   in1=C_R3[:, :], op=ALU.subtract), waits=[t_msq, t_m2], sem=sV)
        t_varc = E_("vector", lambda e: e.tensor_scalar(out=C_R2[:, :], in0=C_R2[:, :],
                    scalar1=zero1[0:1, 0:1], scalar2=None, op0=ALU.max),
                    waits=[t_var], sem=sV)
        t_ln = E_("scalar", lambda e: e.activation(out=C_R2[:, :], in_=C_R2[:, :],
                  func=AF.Ln, bias=eps_t[0:1, 0:1]), waits=[t_varc], sem=sA)
        t_r = E_("scalar", lambda e: e.activation(out=C_R2[:, :], in_=C_R2[:, :],
                 func=AF.Exp, scale=-0.5), waits=[t_ln], sem=sA)
        t_mr = E_("vector", lambda e: e.tensor_tensor(out=C_R1[:, :], in0=C_R1[:, :],
                  in1=C_R2[:, :], op=ALU.mult), waits=[t_r], sem=sV)
        t_b1m = E_("tensor", lambda e: e.matmul(pR2[:, :], ones_f[:, :], C_R2[0:1, :],
                   start=True, stop=True), waits=[t_r], sem=sT)
        t_rb_ = E_("vector", lambda e: e.tensor_copy(out=r2b[:], in_=pR2[:, :]),
                   waits=[t_b1m, cdV], sem=sV)
        t_b2m = E_("tensor", lambda e: e.matmul(pR2[:, :], ones_f[:, :], C_R1[0:1, :],
                   start=True, stop=True), waits=[t_mr, t_rb_], sem=sT)
        t_mb_ = E_("vector", lambda e: e.tensor_copy(out=m2b[:], in_=pR2[:, :]),
                   waits=[t_b2m, cdV], sem=sV)
        apply_toks = []
        ap_war = None
        for dc in range(DC):
            x1sl = x1v[dc // 8][:, (dc % 8) * 512:(dc % 8 + 1) * 512]
            h2sl = h2[:, dc * TOK + q * 512:dc * TOK + (q + 1) * 512]
            t_a1 = E_("vector", lambda e, s=x1sl: e.tensor_tensor(
                out=C_tmp, in0=s, in1=r2b[:], op=ALU.mult),
                waits=[t_rb_, ap_war], sem=sV)
            t_ap = E_("vector", lambda e, hh=h2sl: e.tensor_tensor(
                out=hh, in0=C_tmp, in1=m2b[:], op=ALU.subtract),
                waits=[t_a1, t_mb_], sem=sV)
            ap_war = t_ap
            apply_toks.append(t_ap)
        tok_d_apply += apply_toks
        tok_aff = None
        for dinc in range(DC):
            tok_aff = E_("tensor", lambda e, d=dinc, qq=q: e.matmul(
                pAf[0:16, 0:512], wa_sb[:, d * NE:(d + 1) * NE],
                h2[:, d * TOK + qq * 512:d * TOK + (qq + 1) * 512],
                start=(d == 0), stop=(d == DC - 1)),
                waits=[apply_toks[dinc], t_wa], sem=(sT if dinc == DC - 1 else None))
        t_sg = E_("scalar", lambda e: e.activation(out=aff_f[:, :], in_=pAf[0:16, 0:512],
                  func=AF.Sigmoid), waits=[tok_aff], sem=sA)
        prev_ct = None
        for sub in range(4):
            t_tr = E_("tensor", lambda e, s=sub: e.transpose(
                pTr[:, 0:16], aff_f[0:16, s * 128:(s + 1) * 128], idn_f[0:16, 0:16]),
                waits=[t_sg, t_idf, prev_ct], sem=sT)
            t_at = E_("vector", lambda e: e.tensor_copy(out=afftm[:, :], in_=pTr[:, 0:16]),
                      waits=[t_tr, prev_ct], sem=sV)
            t_m1_ = E_("vector", lambda e: e.tensor_reduce(out=tk_m1[:, :], in_=afftm[:, :],
                       axis=AX.X, op=ALU.max), waits=[t_at], sem=sV)
            t_e1 = E_("vector", lambda e: e.tensor_scalar(out=tk_eq1[:, :], in0=afftm[:, :],
                      scalar1=tk_m1[:, 0:1], scalar2=None, op0=ALU.is_equal),
                      waits=[t_m1_], sem=sV)
            t_a2_ = E_("vector", lambda e: e.scalar_tensor_tensor(
                out=tk_a2[:, :], in0=tk_eq1[:, :], scalar=n1e9[:, 0:1], in1=afftm[:, :],
                op0=ALU.mult, op1=ALU.add), waits=[t_e1, t_n9], sem=sV)
            t_m2_ = E_("vector", lambda e: e.tensor_reduce(out=tk_m2[:, :], in_=tk_a2[:, :],
                       axis=AX.X, op=ALU.max), waits=[t_a2_], sem=sV)
            t_e2 = E_("vector", lambda e: e.tensor_scalar(out=tk_eq2[:, :], in0=tk_a2[:, :],
                      scalar1=tk_m2[:, 0:1], scalar2=None, op0=ALU.is_equal),
                      waits=[t_m2_], sem=sV)
            t_dn = E_("vector", lambda e: e.tensor_tensor(out=tk_d[:, :], in0=tk_m1[:, :],
                      in1=tk_m2[:, :], op=ALU.add), waits=[t_m2_], sem=sV)
            t_rd = E_("vector", lambda e: e.reciprocal(out=tk_d[:, :], in_=tk_d[:, :]),
                      waits=[t_dn], sem=sV)
            t_p1 = E_("vector", lambda e: e.tensor_tensor(out=tk_m1[:, :], in0=tk_m1[:, :],
                      in1=tk_d[:, :], op=ALU.mult), waits=[t_rd], sem=sV)
            t_p2 = E_("vector", lambda e: e.tensor_tensor(out=tk_m2[:, :], in0=tk_m2[:, :],
                      in1=tk_d[:, :], op=ALU.mult), waits=[t_rd], sem=sV)
            t_t1 = E_("vector", lambda e: e.tensor_scalar(out=tk_t1[:, :], in0=tk_eq1[:, :],
                      scalar1=tk_m1[:, 0:1], scalar2=None, op0=ALU.mult),
                      waits=[t_p1, t_e2], sem=sV)
            t_t2 = E_("vector", lambda e: e.tensor_scalar(out=tk_eq2[:, :], in0=tk_eq2[:, :],
                      scalar1=tk_m2[:, 0:1], scalar2=None, op0=ALU.mult),
                      waits=[t_p2], sem=sV)
            t_ct = E_("vector", lambda e: e.tensor_tensor(out=tk_ctm[:, 0:16], in0=tk_t1[:, :],
                      in1=tk_eq2[:, :], op=ALU.add), waits=[t_t1, t_t2], sem=sV)
            t_tb = E_("tensor", lambda e: e.transpose(
                pTr[0:16, :].bitcast(BF16)[0:16, 0:128], tk_ctm[:, 0:16], idn_b[:, :]),
                waits=[t_ct], sem=sT)
            prev_ct = E_("vector", lambda e, qq=q, s=sub: e.tensor_copy(
                out=c_fm[0:16, qq * 512 + s * 128:qq * 512 + (s + 1) * 128],
                in_=pTr[0:16, :].bitcast(BF16)[0:16, 0:128]),
                waits=[t_tb], sem=sV)
        tok_cd_last = prev_ct
    tok_d_done_v = (sV, sV.n)
    tok_d_done_t = (sT, sT.n)

    if dump == "CD":
        _dump(EM, y_v, [(h2[:, 0:16384], 0, 16384)],
              [tok_d_done_v, tok_d_done_t], sSt, sDmp,
              extra=[(c_fm[0:16, :], 16384, 2048)])
        _tail(EM, ALL_DMA_SEMS)
        return nc, EM

    accv = arB[:, :].bitcast(F32)
    y3ar = [arC[:, :], arD[:, :]]
    gblk = [carve(arS, 0, 2048), carve(arS, 2048, 4096)]
    w1blk = [carve(arS, 4096, 6144), carve(arS, 6144, 8192)]
    w2blk = [carve(arS, 8192, 10240), carve(arS, 10240, 12288)]
    ygt = [carve(arS, 12288, 12800), carve(arS, 12800, 13312)]
    cbt = [carve(arS, 13312, 13824), carve(arS, 13824, 14336)]
    c_row = arS[0:1, 14336:16384].bitcast(F32)
    x1f = [carve(arS, 8192, 10240, F32), carve(arS, 10240, 12288, F32)]

    pG = [bank[0], bank[1]]
    pL = [bank[2], bank[3]]
    pO = [bank[4], bank[6]]
    pCB = bank[5]

    tok_g_war = [tok_d_done_t, tok_d_done_t]
    tok_w1_war = [tok_d_done_t, tok_d_done_t]
    tok_w2_war = [tok_d_done_t, tok_d_done_t]
    tok_cd_st = (sSt, sSt.n)
    tok_y3_war = [[tok_d_done_t, tok_cd_st], [tok_d_done_t, tok_cd_st]]
    tok_pg_war = {}
    tok_po_war = {}
    tok_cbt_read = [t_bM, t_bM]
    tok_cb_pcb = t_bM
    tok_rb2_war = None
    for tc2 in range(2):
        tg = tc2 * 1024
        acc_init = [False] * (DC * 2)
        last_out_tok = None
        for ep in range(NEX // 2):
            for el in range(2):
                e_idx = 2 * ep + el
                cb_toks = [None, None]
                if e_idx < NE:
                    t_crow = E_("sync", lambda e, ee=e_idx, t0=tg: e.dma_start(
                        out=c_row[:, :], in_=c_fm[ee:ee + 1, t0:t0 + 1024]),
                        waits=[tok_cd_last, tok_cb_pcb], sem=sCrow, dma=True)
                    for th in range(2):
                        t_cm = E_("tensor", lambda e, h=th: e.matmul(
                            pCB[:, :], ones_f[:, :],
                            c_row[0:1, h * 512:(h + 1) * 512],
                            start=True, stop=True),
                            waits=[t_crow, tok_cb_pcb, tok_cbt_read[th]], sem=sT)
                        cb_toks[th] = E_("vector", lambda e, h=th: e.tensor_copy(
                            out=cbt[h][:], in_=pCB[:, :]), waits=[t_cm], sem=sV)
                        tok_cb_pcb = cb_toks[th]
                for fc in range(FC):
                    i = fc % 2
                    t_g = E_("gpsimd", lambda e, ee=e_idx, f=fc, i2=i: e.dma_start(
                        out=gblk[i2], in_=d_g[ee, f]), waits=[tok_g_war[i]],
                        sem=sG[i], dma=True)
                    t_w = E_("gpsimd", lambda e, ee=e_idx, f=fc, i2=i: e.dma_start(
                        out=w1blk[i2], in_=d_w1[ee, f]), waits=[tok_w1_war[i]],
                        sem=sW1[i], dma=True)
                    for th in range(2):
                        t0 = tg + th * 512
                        tokg = None
                        for dc in range(DC):
                            tokg = E_("tensor", lambda e, d=dc, i2=i, h=th, s=t0: e.matmul(
                                pG[h][:, :], gblk[i2][:, d * 128:(d + 1) * 128],
                                h2[:, d * TOK + s:d * TOK + s + 512],
                                start=(d == 0), stop=(d == DC - 1)),
                                waits=[t_g, tok_d_done_v,
                                       tok_pg_war.get(('g', th))] if dc == 0 else [],
                                sem=(sT if dc == DC - 1 else None))
                        tokl = None
                        for dc in range(DC):
                            tokl = E_("tensor", lambda e, d=dc, i2=i, h=th, s=t0: e.matmul(
                                pL[h][:, :], w1blk[i2][:, d * 128:(d + 1) * 128],
                                h2[:, d * TOK + s:d * TOK + s + 512],
                                start=(d == 0), stop=(d == DC - 1)),
                                waits=[t_w, tok_pg_war.get(('l', th))] if dc == 0 else [],
                                sem=(sT if dc == DC - 1 else None))
                        bcol = e_idx * FC + fc
                        t_yg = E_("scalar", lambda e, h=th, bc=bcol: e.activation(
                            out=ygt[h][:], in_=pG[h][:, :], func=AF.Gelu,
                            bias=gb_sb[:, bc:bc + 1]),
                            waits=[tokg, t_gb, tok_pg_war.get(('yg', th))], sem=sA)
                        tok_pg_war[('g', th)] = t_yg
                        y3sl = y3ar[el][:, fc * 1024 + th * 512:fc * 1024 + (th + 1) * 512]
                        t_y3 = E_("vector", lambda e, h=th, bc=bcol, s=y3sl: e.scalar_tensor_tensor(
                            out=s, in0=pL[h][:, :], scalar=b1_sb[:, bc:bc + 1],
                            in1=ygt[h][:], op0=ALU.add, op1=ALU.mult),
                            waits=[tokl, t_yg, t_b1c] + tok_y3_war[el], sem=sV)
                        tok_pg_war[('l', th)] = t_y3
                        tok_pg_war[('yg', th)] = t_y3
                        if e_idx < NE:
                            t_y3 = E_("vector", lambda e, h=th, s=y3sl: e.tensor_tensor(
                                out=s, in0=s, in1=cbt[h][:], op=ALU.mult),
                                waits=[t_y3, cb_toks[th]], sem=sV)
                            tok_cbt_read[th] = t_y3
                    tok_g_war[i] = (sT, sT.n)
                    tok_w1_war[i] = (sT, sT.n)
            tok_s1_done = (sV, sV.n)
            for dc in range(DC):
                i = dc % 2
                t_w2 = E_("gpsimd", lambda e, p=ep, d=dc, i2=i: e.dma_start(
                    out=w2blk[i2], in_=d_w2[p, d]), waits=[tok_w2_war[i]],
                    sem=sW2[i], dma=True)
                if ep == 0:
                    t_r2l = E_("sync", lambda e, d=dc: e.dma_start(
                        out=rb2_sb[:, :], in_=d_rb2[0:16, d * 128:(d + 1) * 128]),
                        waits=[tok_rb2_war], sem=sRb2, dma=True)
                    t_r2l2 = E_("sync", lambda e, d=dc: e.dma_start(
                        out=sb2_sb[:, :], in_=d_rb2[32:34, d * 128:(d + 1) * 128]),
                        waits=[tok_rb2_war], sem=sRb2, dma=True)
                for th in range(2):
                    nmm = 2 * FC + (2 if ep == 0 else 0)
                    k = 0
                    tok_o = None
                    for el in range(2):
                        for fc in range(FC):
                            first = (k == 0)
                            last = (k == nmm - 1)
                            w0 = [t_w2, tok_s1_done, tok_po_war.get(th),
                                  last_out_tok] if first else []
                            tok_o = E_("tensor", lambda e, l=el, f=fc, h=th, i2=i,
                                       st=first, sp=last: e.matmul(
                                pO[h][:, :],
                                w2blk[i2][:, (l * FC + f) * 128:(l * FC + f + 1) * 128],
                                y3ar[l][:, f * 1024 + h * 512:f * 1024 + (h + 1) * 512],
                                start=st, stop=sp, skip_group_check=True),
                                waits=w0, sem=(sT if last else None))
                            k += 1
                    if ep == 0:
                        tok_o = E_("tensor", lambda e, h=th, t0=tg: e.matmul(
                            pO[h][:, :], rb2_sb[:, :],
                            c_fm[0:16, t0 + h * 512:t0 + (h + 1) * 512],
                            start=False, stop=False, skip_group_check=True),
                            waits=[t_r2l2, tok_cd_last])
                        k += 1
                        tok_o = E_("tensor", lambda e, h=th: e.matmul(
                            pO[h][:, :], sb2_sb[:, :], ones2f[:, :],
                            start=False, stop=True, skip_group_check=True),
                            waits=[t_on2, t_r2l2], sem=sT)
                    accsl = accv[:, dc * 1024 + th * 512:dc * 1024 + (th + 1) * 512]
                    if not acc_init[dc * 2 + th]:
                        t_ac = E_("vector", lambda e, a=accsl, h=th: e.tensor_copy(
                            out=a, in_=pO[h][:, :]),
                            waits=[tok_o, tok_b_done,
                                   tok_po_war.get(('prev_tc', dc, th))], sem=sV)
                        acc_init[dc * 2 + th] = True
                    else:
                        t_ac = E_("vector", lambda e, a=accsl, h=th: e.tensor_tensor(
                            out=a, in0=a, in1=pO[h][:, :], op=ALU.add),
                            waits=[tok_o], sem=sV)
                    tok_po_war[th] = t_ac
                tok_w2_war[i] = (sT, sT.n)
                if ep == 0:
                    tok_rb2_war = (sT, sT.n)
            tok_y3_war = [[(sT, sT.n)], [(sT, sT.n)]]
        prev_x1 = [tok_g_war[0], tok_g_war[1]]
        fin_toks = {}
        for dc in range(DC):
            i = dc % 2
            t_xl = E_("sync", lambda e, d=dc, t0=tg, i2=i: e.dma_start(
                out=x1f[i2], in_=d_x1s[d * 128:(d + 1) * 128, t0:t0 + 1024]),
                waits=[prev_x1[i], (sT, sT.n) if dc < 2 else None,
                       (sSt, sSt.n) if dc < 2 else None], sem=sX1f[i], dma=True)
            accsl = accv[:, dc * 1024:(dc + 1) * 1024]
            t_fh = E_("vector", lambda e, a=accsl, d=dc, t0=tg: e.tensor_tensor(
                out=a, in0=a, in1=h2[:, d * TOK + t0:d * TOK + t0 + 1024],
                op=ALU.add),
                waits=[tok_po_war.get(0), tok_po_war.get(1)], sem=sV)
            t_fin = E_("vector", lambda e, a=accsl, i2=i: e.tensor_tensor(
                out=a, in0=a, in1=x1f[i2][:], op=ALU.add),
                waits=[t_xl, t_fh], sem=sV)
            prev_x1[i] = t_fin
            tok_st = E_("sync", lambda e, a=accsl, d=dc, t0=tg: e.dma_start(
                out=d_y[d * 128:(d + 1) * 128, t0:t0 + 1024], in_=a),
                waits=[t_fin], sem=sSt, dma=True)
            fin_toks[dc] = tok_st
            last_out_tok = tok_st
        for dc in range(DC):
            for th in range(2):
                tok_po_war[('prev_tc', dc, th)] = fin_toks[DC - 1]
        tok_g_war = [(sT, sT.n), (sT, sT.n)]
        tok_w1_war = [tok_g_war[0], tok_g_war[1]]
        tok_w2_war = [prev_x1[0], prev_x1[1]]
    _tail(EM, ALL_DMA_SEMS)
    return nc, EM


def _tail(EM, sems):
    for sm in sems:
        EM.step("sync", lambda e, s=sm: e.wait_ge(s.h, s.n), [])


def _dump(EM, y_v, items, waits, sSt, sDmp, extra=(), fitems=()):
    for ap, ci in fitems:
        n = ap.shape[-1]
        EM.step("sync", lambda e, a=ap, c=ci, nn=n: e.dma_start(
            out=y_v[:, c, 0:nn], in_=a), list(waits), sSt, True)
    for ap, col, n in items:
        c0, nch = col // 2048, n // 2048
        EM.step("gpsimd", lambda e, a=ap, c=c0, k=nch: e.dma_start(
            out=y_v[:, c:c + k, :],
            in_=a.rearrange("p (c t) -> p c t", c=k)), list(waits), sDmp, True)
    for ap, col, n in extra:
        c0, nch = col // 2048, n // 2048
        EM.step("gpsimd", lambda e, a=ap, c=c0, k=nch: e.dma_start(
            out=y_v[0:16, c:c + k, :],
            in_=a.rearrange("p (c t) -> p c t", c=k)), list(waits), sDmp, True)



def finalize(nc, EM):
    with nc.Block() as block:
        for name in ENGS:
            steps = EM.steps[name]

            def mk(steps):
                def body(eng):
                    for fn, ws, sem, amt in steps:
                        for (s, v) in ws:
                            eng.wait_ge(s.h, v)
                        ins = fn(eng)
                        if sem is not None and ins is not None:
                            ins.then_inc(sem.h, amt)
                return body
            getattr(block, name)(mk(steps))
    return nc



def _host_prep(inputs):
    x = np.asarray(inputs["L0_attn_x"], dtype=np.float32)
    w_attn = np.asarray(inputs["L0_attn_w_attn"], dtype=np.float32)
    w_out = np.asarray(inputs["L0_attn_w_out"], dtype=np.float32)
    wa = np.asarray(inputs["L0_ffn_wa"], dtype=np.float32)
    rg = np.asarray(inputs["L0_ffn_rg"], dtype=np.float32)
    rgb = np.asarray(inputs["L0_ffn_rgb"], dtype=np.float32)
    rw1 = np.asarray(inputs["L0_ffn_rw1"], dtype=np.float32)
    rb1 = np.asarray(inputs["L0_ffn_rb1"], dtype=np.float32)
    rw2 = np.asarray(inputs["L0_ffn_rw2"], dtype=np.float32)
    rb2 = np.asarray(inputs["L0_ffn_rb2"], dtype=np.float32)
    sg = np.asarray(inputs["L0_ffn_sg"], dtype=np.float32)
    sgb = np.asarray(inputs["L0_ffn_sgb"], dtype=np.float32)
    sw1 = np.asarray(inputs["L0_ffn_sw1"], dtype=np.float32)
    sb1 = np.asarray(inputs["L0_ffn_sb1"], dtype=np.float32)
    sw2 = np.asarray(inputs["L0_ffn_sw2"], dtype=np.float32)
    sb2 = np.asarray(inputs["L0_ffn_sb2"], dtype=np.float32)

    c = np.ascontiguousarray

    pos = np.asarray(10000.0 ** ((-2.0 * np.arange(0, HD, 2) - 1.0) / HD),
                     dtype=np.float32)
    angles = np.arange(T, dtype=np.float32)[:, None] * pos[None, :]
    rot = np.concatenate((angles, angles), axis=-1)
    cosT = c(np.cos(rot).astype(np.float32).T)
    sinT = c(np.sin(rot).astype(np.float32).T)

    waT = w_attn.T
    waTt = waT.reshape(DC, 128, NH + 2, 128).transpose(1, 0, 2, 3)
    waTkv = c(waTt[:, :, NH:, :].reshape(128, DC * 2 * 128))
    waTq = c(waTt[:, :, :NH, :].transpose(0, 2, 1, 3).reshape(128, NH * DC * 128))

    woT = w_out.T
    woTt = c(woT.reshape(NH, 128, DC, 128).transpose(2, 1, 0, 3).reshape(DC, 128, D))

    wa_t = c(wa.reshape(DC, 128, NE).transpose(1, 0, 2).reshape(128, DC * NE))

    A_g = np.concatenate([rg, sg], axis=0)
    A_w1 = np.concatenate([rw1, sw1], axis=0)
    A_w2 = np.concatenate([rw2, sw2], axis=0)
    g_t = c(A_g.reshape(NEX, DC, 128, FC, 128).transpose(0, 3, 2, 1, 4)
            .reshape(NEX, FC, 128, DC * 128))
    w1_t = c(A_w1.reshape(NEX, DC, 128, FC, 128).transpose(0, 3, 2, 1, 4)
             .reshape(NEX, FC, 128, DC * 128))
    w2t = A_w2.reshape(NEX, FC, 128, DC, 128).transpose(0, 3, 2, 1, 4)
    w2p = c(w2t.reshape(NEX // 2, 2, DC, 128, FC * 128)
            .transpose(0, 2, 3, 1, 4).reshape(NEX // 2, DC, 128, 2048))

    gball = np.concatenate([rgb, sgb], axis=0)
    b1all = np.concatenate([rb1, sb1], axis=0)
    gb_t = c(gball.reshape(NEX, FC, 128).transpose(2, 0, 1).reshape(128, NEX * FC))
    b1_t = c(b1all.reshape(NEX, FC, 128).transpose(2, 0, 1).reshape(128, NEX * FC))

    rb2sb2 = np.zeros((34, D), dtype=np.float32)
    rb2sb2[0:16] = rb2
    rb2sb2[32:34] = sb2

    idn = np.eye(128, dtype=np.float32)

    shared = dict(cosT=cosT, sinT=sinT, waTkv=waTkv, waTq=waTq, woTt=woTt,
                  wa=wa_t, g_t=g_t, w1_t=w1_t, w2p_t=w2p, gb_t=gb_t, b1_t=b1_t,
                  rb2sb2=rb2sb2, idn=idn)

    xTb = [c(x[b].T) for b in range(B)]
    in_maps = []
    for core in range(8):
        b = core // 4
        q0 = (core % 4) * TOK
        m = dict(shared)
        m["xT"] = xTb[b]
        m["xTown"] = c(xTb[b][:, q0:q0 + TOK])
        m["cosO"] = c(cosT[:, q0:q0 + TOK])
        m["sinO"] = c(sinT[:, q0:q0 + TOK])
        in_maps.append(m)
    return in_maps


def kernel(**inputs):
    in_maps = _host_prep(inputs)
    nc, EM = build_program(dump=None)
    nc = finalize(nc, EM)
    res = run_bass_kernel_spmd(nc, in_maps, core_ids=list(range(8)))
    out = np.empty((B, T, D), dtype=np.float32)
    for core in range(8):
        b = core // 4
        q0 = (core % 4) * TOK
        out[b, q0:q0 + TOK, :] = res.results[core]["y"].T
    return out


# version 06
# speedup vs baseline: 1.1033x; 1.1033x over previous
import numpy as np
import concourse.bass as bass
import concourse.mybir as mybir
from concourse.bass_utils import run_bass_kernel_spmd

F32 = mybir.dt.float32
BF16 = mybir.dt.bfloat16
AF = mybir.ActivationFunctionType
ALU = mybir.AluOpType
AX = mybir.AxisListType

B, T, D = 2, 8192, 2048
NH, HD = 16, 128
NE, NSH, FF = 16, 2, 1024
NEX = NE + NSH
TOK = 2048
DC = D // 128
FC = FF // 128
KCN = T // 128
NTC = T // 512
NQC = TOK // 512
EPS = 1e-5
ISQ = float(1.0 / np.sqrt(128.0))

ENGS = ("sync", "gpsimd", "tensor", "scalar", "vector")


class Sem:
    __slots__ = ("h", "n", "name")

    def __init__(self, nc, name):
        self.h = nc.alloc_semaphore(name)
        self.n = 0
        self.name = name


class Emit:
    def __init__(self):
        self.steps = {e: [] for e in ENGS}
        self.hwm = {e: {} for e in ENGS}

    def step(self, eng, fn, waits=(), sem=None, dma=False, force=False):
        ws = []
        for w in waits:
            if w is None:
                continue
            s, v = w
            if force or self.hwm[eng].get(s, -1) < v:
                if self.hwm[eng].get(s, -1) < v:
                    self.hwm[eng][s] = v
                ws.append((s, v))
        tokn = None
        amt = 0
        if sem is not None:
            amt = 16 if dma else 1
            sem.n += amt
            tokn = (sem, sem.n)
        self.steps[eng].append((fn, ws, sem, amt))
        return tokn


DBG_NTC = NTC
DBG_AQ = 2
DBG_BH = NH


def build_program(dump=None):
    nc = bass.Bass("TRN2", target_bir_lowering=False, debug=False)

    d_xT = nc.dram_tensor("xT", [D, T], F32, kind="ExternalInput").ap()
    d_xTown = nc.dram_tensor("xTown", [D, TOK], F32, kind="ExternalInput").ap()
    d_cosT = nc.dram_tensor("cosT", [HD, T], F32, kind="ExternalInput").ap()
    d_sinT = nc.dram_tensor("sinT", [HD, T], F32, kind="ExternalInput").ap()
    d_cosO = nc.dram_tensor("cosO", [HD, TOK], F32, kind="ExternalInput").ap()
    d_sinO = nc.dram_tensor("sinO", [HD, TOK], F32, kind="ExternalInput").ap()
    d_waTkv = nc.dram_tensor("waTkv", [128, DC * 2 * 128], F32, kind="ExternalInput").ap()
    d_waTq = nc.dram_tensor("waTq", [128, DC * NH * 128], F32, kind="ExternalInput").ap()
    d_woTt = nc.dram_tensor("woTt", [DC, 128, D], F32, kind="ExternalInput").ap()
    d_wa = nc.dram_tensor("wa", [128, DC * NE], F32, kind="ExternalInput").ap()
    d_g = nc.dram_tensor("g_t", [NEX, FC, 128, DC * 128], F32, kind="ExternalInput").ap()
    d_w1 = nc.dram_tensor("w1_t", [NEX, FC, 128, DC * 128], F32, kind="ExternalInput").ap()
    d_w2 = nc.dram_tensor("w2p_t", [NEX // 2, DC, 128, 2048], F32, kind="ExternalInput").ap()
    d_gb = nc.dram_tensor("gb_t", [128, NEX * FC], F32, kind="ExternalInput").ap()
    d_b1 = nc.dram_tensor("b1_t", [128, NEX * FC], F32, kind="ExternalInput").ap()
    d_rb2 = nc.dram_tensor("rb2sb2", [34, D], F32, kind="ExternalInput").ap()
    d_idn = nc.dram_tensor("idn", [128, 128], F32, kind="ExternalInput").ap()
    d_y = nc.dram_tensor("y", [D, TOK], F32, kind="ExternalOutput").ap()
    d_x1s = nc.dram_tensor("x1s", [D, TOK], F32, kind="Internal").ap()
    y_v = d_y.rearrange("(c p) t -> p c t", p=128)

    arA = nc.alloc_sbuf_tensor("arA", [128, 32768], BF16)
    arB = nc.alloc_sbuf_tensor("arB", [128, 32768], BF16)
    arC = nc.alloc_sbuf_tensor("arC", [128, 8192], BF16)
    arD = nc.alloc_sbuf_tensor("arD", [128, 8192], BF16)
    arS = nc.alloc_sbuf_tensor("arS", [128, 16384], BF16)

    ones_f = nc.alloc_sbuf_tensor("ones_f", [1, 128], F32)
    ones_fc = nc.alloc_sbuf_tensor("ones_fc", [128, 1], F32)
    ones_b = nc.alloc_sbuf_tensor("ones_b", [128, 1], BF16)
    ones2f = nc.alloc_sbuf_tensor("ones2f", [2, 512], F32)
    idn_f = nc.alloc_sbuf_tensor("idn_f", [128, 128], F32)
    idn_b = nc.alloc_sbuf_tensor("idn_b", [128, 128], BF16)
    eps_t = nc.alloc_sbuf_tensor("eps_t", [1, 1], F32)
    zero1 = nc.alloc_sbuf_tensor("zero1", [1, 1], F32)
    n1e9 = nc.alloc_sbuf_tensor("n1e9", [128, 1], F32)
    gb_sb = nc.alloc_sbuf_tensor("gb_sb", [128, NEX * FC], F32)
    b1_sb = nc.alloc_sbuf_tensor("b1_sb", [128, NEX * FC], F32)
    wa_sb = nc.alloc_sbuf_tensor("wa_sb", [128, DC * NE], BF16)
    biasM = nc.alloc_sbuf_tensor("biasM", [128, 1], F32)
    c_fm = nc.alloc_sbuf_tensor("c_fm", [16, TOK], F32)
    rb2_sb = nc.alloc_sbuf_tensor("rb2_sb", [16, 128], F32)
    sb2_sb = nc.alloc_sbuf_tensor("sb2_sb", [2, 128], F32)
    kmax = nc.alloc_sbuf_tensor("kmax", [1, 1], F32)
    qmax = nc.alloc_sbuf_tensor("qmax", [1, 1], F32)
    sc1 = nc.alloc_sbuf_tensor("sc1", [1, 1], F32)
    sc2 = nc.alloc_sbuf_tensor("sc2", [1, 1], F32)

    def carve(arena, c0, c1, dt=None):
        v = arena[:, c0:c1]
        if dt is F32:
            v = v.bitcast(F32)
        return v

    pq01 = nc.alloc_psum_tensor("pq01", [128, 1024], F32)
    pq23 = nc.alloc_psum_tensor("pq23", [128, 1024], F32)
    pq45 = nc.alloc_psum_tensor("pq45", [128, 1024], F32)
    ps6 = nc.alloc_psum_tensor("ps6", [128, 512], F32)
    ps7 = nc.alloc_psum_tensor("ps7", [128, 512], F32)
    bank = [ps6[:, :], ps7[:, :], pq45[:, 0:512], pq45[:, 512:1024],
            pq01[:, 0:512], pq01[:, 512:1024],
            pq23[:, 0:512], pq23[:, 512:1024]]

    sSt = Sem(nc, "sSt")
    sT = Sem(nc, "sT")
    sA = Sem(nc, "sA")
    sV = Sem(nc, "sV")
    sCst = Sem(nc, "sCst")
    sCsS = Sem(nc, "sCsS")
    sWq = Sem(nc, "sWq")
    sCos = Sem(nc, "sCos")
    sSin = Sem(nc, "sSin")
    sXt = Sem(nc, "sXt")
    sRb2 = Sem(nc, "sRb2")
    sCrow = Sem(nc, "sCrow")
    sXb = [Sem(nc, "sXb0"), Sem(nc, "sXb1")]
    sWo = [Sem(nc, "sWo0"), Sem(nc, "sWo1")]
    sG = [Sem(nc, "sG0"), Sem(nc, "sG1")]
    sW1 = [Sem(nc, "sW10"), Sem(nc, "sW11")]
    sW2 = [Sem(nc, "sW20"), Sem(nc, "sW21")]
    sX1f = [Sem(nc, "sX1f0"), Sem(nc, "sX1f1")]
    sEp = Sem(nc, "sEp")
    sDmp = Sem(nc, "sDmp")
    ALL_DMA_SEMS = ([sSt, sDmp, sCst, sCsS, sWq, sCos, sSin, sXt, sRb2, sCrow]
                    + sXb + sWo + sG + sW1 + sW2 + sX1f)
    EM = Emit()

    def E_(eng, fn, waits=(), sem=None, dma=False, force=False):
        return EM.step(eng, fn, waits, sem, dma, force)

    E_("sync", lambda e: e.dma_start(out=gb_sb[:], in_=d_gb[:, :]), sem=sCsS, dma=True)
    E_("sync", lambda e: e.dma_start(out=b1_sb[:], in_=d_b1[:, :]), sem=sCsS, dma=True)
    E_("sync", lambda e: e.dma_start(out=idn_f[:], in_=d_idn[:, :]), sem=sCsS, dma=True)
    E_("gpsimd", lambda e: e.dma_start(out=idn_b[:], in_=d_idn[:, :]), sem=sCst, dma=True)
    E_("gpsimd", lambda e: e.dma_start(out=wa_sb[:], in_=d_wa[:, :]), sem=sCst, dma=True)
    E_("gpsimd", lambda e: e.dma_start(out=carve(arS, 0, 4096), in_=d_waTkv[:, :]), sem=sCst, dma=True)
    t_gb = t_b1c = t_idf = (sCsS, sCsS.n)
    t_idb = t_wa = t_wkv = (sCst, sCst.n)
    t_onf = E_("vector", lambda e: e.memset(ones_f[:], 1.0), sem=sV)
    t_onfc = E_("vector", lambda e: e.memset(ones_fc[:], 1.0), sem=sV)
    t_onb = E_("vector", lambda e: e.memset(ones_b[:], 1.0), sem=sV)
    t_on2 = E_("vector", lambda e: e.memset(ones2f[:], 1.0), sem=sV)
    t_eps = E_("vector", lambda e: e.memset(eps_t[:], EPS), sem=sV)
    t_z1 = E_("vector", lambda e: e.memset(zero1[:], 0.0), sem=sV)
    t_n9 = E_("vector", lambda e: e.memset(n1e9[:], -1.0e9), sem=sV)

    S_t1 = carve(arS, 4096, 5120, F32)
    S_t2 = carve(arS, 5120, 6144, F32)
    S_rb = carve(arS, 6144, 7168, F32)
    S_cosr = carve(arS, 7168, 8192, F32)
    S_sinr = carve(arS, 8192, 9216, F32)
    S_vtmp = carve(arS, 9216, 9728)
    S_xsq = [carve(arS, 9728 + 512 * i, 9728 + 512 * (i + 1)) for i in range(2)]
    S_R3 = arS[0:1, 10752:11776].bitcast(F32)
    S_cosc = carve(arS, 11776, 12800, F32)
    S_sinc = carve(arS, 12800, 13824, F32)
    S_wsum = arS[0:1, 13824:14336].bitcast(F32)
    S_R1 = arS[0:1, 14336:15360].bitcast(F32)
    S_R2 = arS[0:1, 15360:16384].bitcast(F32)
    Q_wsum = arS[0:1, 0:4096].bitcast(F32)

    xb_bufs = [carve(arB, 0, 8192), carve(arB, 8192, 16384)]
    waTq_sb = carve(arB, 16384, 32768)
    K_fm = arC[:, :]
    V_tm = arD[:, :]
    Q_sb = arA[:, :]

    ST = {
        "stats_war": None, "sq_war": {}, "rb_war": None, "cos_war": None,
        "qkv_war": {}, "qkv_rot": 0, "v_war": None, "tr_war": None,
    }

    def a_chunk(xb, x_dram_view, wsum_row, dout_list, lhsT_for, evac,
                tok_prev_use, cosd, sind, xbsem):
        pS, pQ = bank[2], bank[3]
        t_ld = E_("gpsimd", lambda e, o=xb, i=x_dram_view: e.dma_start(
            out=o.rearrange("p (c t) -> p c t", c=16), in_=i),
            waits=[tok_prev_use], sem=xbsem, dma=True)
        t_cos = E_("sync", lambda e, i=cosd: e.dma_start(out=S_cosc, in_=i),
                   waits=[ST["cos_war"]], sem=sCos, dma=True)
        t_sin = E_("sync", lambda e, i=sind: e.dma_start(out=S_sinc, in_=i),
                   waits=[ST["cos_war"]], sem=sSin, dma=True)
        tok_sum = None
        for dinc in range(DC):
            tok_sum = E_("tensor", lambda e, d=dinc, x=xb: e.matmul(
                pS[0:1, :], ones_b[:, :], x[:, d * 512:(d + 1) * 512],
                start=(d == 0), stop=(d == DC - 1)),
                waits=[t_ld, ST["stats_war"], t_onb],
                sem=(sT if dinc == DC - 1 else None))
        for dinc in range(DC):
            t_sq = E_("vector", lambda e, d=dinc, x=xb: e.tensor_tensor(
                out=S_xsq[d % 2][:], in0=x[:, d * 512:(d + 1) * 512],
                in1=x[:, d * 512:(d + 1) * 512], op=ALU.mult),
                waits=[t_ld, ST["sq_war"].get(dinc % 2)], sem=sV)
            ST["sq_war"][dinc % 2] = E_("tensor", lambda e, d=dinc: e.matmul(
                pQ[0:1, :], ones_b[:, :], S_xsq[d % 2][:],
                start=(d == 0), stop=(d == DC - 1)),
                waits=[t_sq, ST["stats_war"]], sem=sT)
        tok_sq = ST["sq_war"][(DC - 1) % 2]
        pcT, pcV = ST.get("pcT"), ST.get("pcV")
        t_mean = E_("scalar", lambda e: e.mul(S_R1[:, :], pS[0:1, :], 1.0 / 2048.0),
                    waits=[tok_sum, pcT, pcV], sem=sA)
        t_msq = E_("scalar", lambda e: e.mul(S_R2[:, :], pQ[0:1, :], 1.0 / 2048.0),
                   waits=[tok_sq, pcT, pcV], sem=sA)
        ST["stats_war"] = t_msq
        t_m2 = E_("vector", lambda e: e.tensor_tensor(out=S_R3[:, :], in0=S_R1[:, :],
                  in1=S_R1[:, :], op=ALU.mult), waits=[t_mean, pcV], sem=sV)
        t_var = E_("vector", lambda e: e.tensor_tensor(out=S_R2[:, :], in0=S_R2[:, :],
                   in1=S_R3[:, :], op=ALU.subtract), waits=[t_msq, t_m2], sem=sV)
        t_varc = E_("vector", lambda e: e.tensor_scalar(out=S_R2[:, :], in0=S_R2[:, :],
                    scalar1=zero1[0:1, 0:1], scalar2=None, op0=ALU.max),
                    waits=[t_var, t_z1], sem=sV)
        t_ln = E_("scalar", lambda e: e.activation(out=S_R2[:, :], in_=S_R2[:, :],
                  func=AF.Ln, bias=eps_t[0:1, 0:1]), waits=[t_varc, t_eps], sem=sA)
        t_r = E_("scalar", lambda e: e.activation(out=S_R2[:, :], in_=S_R2[:, :],
                 func=AF.Exp, scale=-0.5), waits=[t_ln], sem=sA)
        t_mr = E_("vector", lambda e: e.tensor_tensor(out=S_R1[:, :], in0=S_R1[:, :],
                  in1=S_R2[:, :], op=ALU.mult), waits=[t_r], sem=sV)
        t_rbm = E_("tensor", lambda e: e.matmul(bank[6][:, :], ones_f[:, :],
                   S_R2[0:1, :], start=True, stop=True),
                   waits=[t_r, t_onf, ST["rb_war"]], sem=sT)
        t_rb = E_("vector", lambda e: e.tensor_copy(out=S_rb[:], in_=bank[6][:, :]),
                  waits=[t_rbm, pcV], sem=sV)
        ST["rb_war"] = t_rb
        t_cr = E_("vector", lambda e: e.tensor_tensor(out=S_cosr[:], in0=S_cosc[:],
                  in1=S_rb[:], op=ALU.mult), waits=[t_cos, t_rb, pcV], sem=sV)
        t_sr = E_("vector", lambda e: e.tensor_tensor(out=S_sinr[:], in0=S_sinc[:],
                  in1=S_rb[:], op=ALU.mult), waits=[t_sin, t_rb, pcV], sem=sV)
        ST["cos_war"] = t_sr
        for j in dout_list:
            pb = bank[ST["qkv_rot"] % 2]
            key = ST["qkv_rot"] % 2
            ST["qkv_rot"] += 1
            war = ST["qkv_war"].get(key)
            for dinc in range(DC):
                E_("tensor", lambda e, d=dinc, jj=j, p=pb, x=xb: e.matmul(
                    p[:, :], lhsT_for(jj, d), x[:, d * 512:(d + 1) * 512],
                    start=(d == 0), stop=False),
                    waits=[t_ld, war] if dinc == 0 else [])
            tok_mm = E_("tensor", lambda e, jj=j, p=pb: e.matmul(
                p[:, :], wsum_row[0:1, jj * 128:(jj + 1) * 128], S_R1[0:1, :],
                start=False, stop=True), waits=[t_mr], sem=sT)
            ST["qkv_war"][key] = evac(j, pb, tok_mm, t_cr, t_sr, t_rb)
        ST["pcT"] = (sT, sT.n)
        ST["pcV"] = (sV, sV.n)
        return t_ld

    def rope_evac(dst, psO, tok_mm, t_cr, t_sr):
        war = ST.get("rope_war")
        t_a = E_("vector", lambda e: e.tensor_tensor(out=S_t1[:], in0=psO[:, :],
                 in1=S_cosr[:], op=ALU.mult), waits=[tok_mm, t_cr, war], sem=sV)
        t_b = E_("vector", lambda e: e.tensor_tensor(out=S_t2[0:64, :], in0=psO[64:128, :],
                 in1=S_sinr[0:64, :], op=ALU.mult), waits=[tok_mm, t_sr, war], sem=sV)
        t_c = E_("vector", lambda e: e.tensor_tensor(out=S_t2[64:128, :], in0=psO[0:64, :],
                 in1=S_sinr[64:128, :], op=ALU.mult), waits=[tok_mm, t_sr, war], sem=sV)
        E_("vector", lambda e, d=dst: e.tensor_tensor(out=d[0:64, :], in0=S_t1[0:64, :],
           in1=S_t2[0:64, :], op=ALU.subtract), waits=[t_a, t_b], sem=sV)
        t_e = E_("vector", lambda e, d=dst: e.tensor_tensor(out=d[64:128, :],
                 in0=S_t1[64:128, :], in1=S_t2[64:128, :], op=ALU.add),
                 waits=[t_a, t_c], sem=sV)
        ST["rope_war"] = t_e
        return t_e

    wkv = carve(arS, 0, 4096)
    tok_wm = None
    for kv in range(2):
        for dinc in range(DC):
            tok_wm = E_("tensor", lambda e, d=dinc, k=kv: e.matmul(
                bank[4][0:1, k * 128:(k + 1) * 128], ones_b[:, :],
                wkv[:, (d * 2 + k) * 128:(d * 2 + k + 1) * 128],
                start=(d == 0), stop=(d == DC - 1)),
                waits=[t_wkv, t_onb], sem=(sT if dinc == DC - 1 else None))
    t_wsum = E_("vector", lambda e: e.tensor_scalar(
        out=S_wsum[:, :], in0=bank[4][0:1, 0:256], scalar1=-1.0,
        scalar2=None, op0=ALU.mult), waits=[tok_wm], sem=sV)
    ST["tr_war"] = t_wsum

    tok_xb_war = [None, None]
    for tc in range(DBG_NTC):
        sl = slice(tc * 512, (tc + 1) * 512)
        xb = xb_bufs[tc % 2]
        xdv = d_xT.rearrange("(c p) t -> p c t", p=128)[:, :, sl]

        def evac_kv(j, psO, tok_mm, t_cr, t_sr, t_rb, tc=tc):
            if j == 0:
                return rope_evac(K_fm[:, tc * 512:(tc + 1) * 512], psO, tok_mm, t_cr, t_sr)
            t_v = E_("vector", lambda e: e.tensor_tensor(
                out=S_vtmp[:], in0=psO[:, :], in1=S_rb[:], op=ALU.mult),
                waits=[tok_mm, t_rb, ST["v_war"]], sem=sV)
            ptr = bank[4].bitcast(BF16)
            ttr = None
            for i in range(4):
                ttr = E_("tensor", lambda e, ii=i: e.transpose(
                    ptr[:, ii * 128:(ii + 1) * 128],
                    S_vtmp[:, ii * 128:(ii + 1) * 128], idn_b[:, :]),
                    waits=[t_v, t_idb, ST["tr_war"]], sem=(sT if i == 3 else None))
            t_vc = E_("vector", lambda e: e.tensor_copy(
                out=V_tm[:, tc * 512:(tc + 1) * 512], in_=ptr[:, 0:512]),
                waits=[ttr], sem=sV)
            ST["v_war"] = t_vc
            ST["tr_war"] = t_vc
            return t_vc

        def lhsT_kv(j, dinc):
            return wkv[:, (dinc * 2 + j) * 128:(dinc * 2 + j + 1) * 128]

        a_chunk(xb, xdv, S_wsum, [0, 1], lhsT_kv, evac_kv,
                tok_xb_war[tc % 2], d_cosT[:, sl], d_sinT[:, sl], sXb[tc % 2])
        tok_xb_war[tc % 2] = (sT, sT.n)
    tok_akv_T = (sT, sT.n)

    for qh in range(DBG_AQ):
        t_wq = E_("gpsimd", lambda e, q0=qh: e.dma_start(
            out=waTq_sb, in_=d_waTq[:, q0 * 16384:(q0 + 1) * 16384]),
            waits=[(sT, sT.n)], sem=sWq, dma=True)
        prev_ts = (sV, sV.n)
        for jl in range(8):
            tok_wm = None
            for dinc in range(DC):
                tok_wm = E_("tensor", lambda e, d=dinc, j=jl: e.matmul(
                    bank[4][0:1, 0:128], ones_b[:, :],
                    waTq_sb[:, (j * DC + d) * 128:(j * DC + d + 1) * 128],
                    start=(d == 0), stop=(d == DC - 1)),
                    waits=[t_wq, prev_ts if dinc == 0 else None,
                           ST["tr_war"] if jl == 0 else None],
                    sem=(sT if dinc == DC - 1 else None))
            prev_ts = E_("vector", lambda e, j=jl: e.tensor_scalar(
                out=Q_wsum[:, j * 128:(j + 1) * 128], in0=bank[4][0:1, 0:128],
                scalar1=-1.0, scalar2=None, op0=ALU.mult),
                waits=[tok_wm, tok_akv_T], sem=sV)
        tok_qw_v = (sV, sV.n)

        for tq in range(NQC):
            sl = slice(tq * 512, (tq + 1) * 512)
            xb = xb_bufs[tq % 2]
            xdv = d_xTown.rearrange("(c p) t -> p c t", p=128)[:, :, sl]

            def evac_q(j, psO, tok_mm, t_cr, t_sr, t_rb, tq=tq, qh=qh):
                hg = qh * 8 + j
                dst = Q_sb[:, hg * TOK + tq * 512: hg * TOK + (tq + 1) * 512]
                return rope_evac(dst, psO, tok_mm, t_cr, t_sr)

            def lhsT_q(j, dinc):
                return waTq_sb[:, (j * DC + dinc) * 128:(j * DC + dinc + 1) * 128]

            if tq == 0:
                for key in (0, 1):
                    prev = ST["qkv_war"].get(key)
                    if prev is None or (prev[0] is tok_qw_v[0] and prev[1] < tok_qw_v[1]):
                        ST["qkv_war"][key] = tok_qw_v
            a_chunk(xb, xdv, Q_wsum, list(range(8)), lhsT_q, evac_q,
                    tok_xb_war[tq % 2], d_cosO[:, sl], d_sinO[:, sl], sXb[tq % 2])
            tok_xb_war[tq % 2] = (sT, sT.n)
    tok_aq_T = (sT, sT.n)
    tok_aq_V = (sV, sV.n)

    if dump == "A":
        _dump(EM, y_v, [(K_fm, 0, 8192), (V_tm, 8192, 8192),
                        (Q_sb[:, 0:4096], 16384, 4096),
                        (Q_sb[:, 15 * TOK:16 * TOK], 20480, 2048)],
              [tok_aq_T, tok_aq_V], sSt, sDmp,
              fitems=[(S_rb, 12), (S_cosr, 13), (S_sinr, 14), (S_cosc, 15)])
        _tail(EM, ALL_DMA_SEMS)
        return nc, EM

    ksq = carve(arS, 0, 8192)
    t_ksq = E_("vector", lambda e: e.tensor_tensor(out=ksq[:], in0=K_fm[:], in1=K_fm[:],
               op=ALU.mult), waits=[tok_aq_T, tok_aq_V], sem=sV)
    E_("vector", lambda e: e.memset(kmax[:], 0.0), sem=sV)
    E_("vector", lambda e: e.memset(qmax[:], 0.0), sem=sV)
    t_mx0 = (sV, sV.n)
    prev_red = t_mx0
    for j in range(16):
        pm = bank[j % 2]
        t_mm = E_("tensor", lambda e, jj=j, p=pm: e.matmul(
            p[0:1, :], ones_b[:, :], ksq[:, jj * 512:(jj + 1) * 512],
            start=True, stop=True), waits=[t_ksq, prev_red], sem=sT)
        t_red = E_("vector", lambda e, p=pm: e.tensor_reduce(
            out=sc1[:, :], in_=p[0:1, :], axis=AX.X, op=ALU.max),
            waits=[t_mm, prev_red], sem=sV)
        prev_red = E_("vector", lambda e: e.tensor_tensor(
            out=kmax[:, :], in0=kmax[:, :], in1=sc1[:, :], op=ALU.max),
            waits=[t_red], sem=sV)
    for piece in range(4):
        t_qsq = E_("vector", lambda e, pp=piece: e.tensor_tensor(
            out=ksq[:], in0=Q_sb[:, pp * 8192:(pp + 1) * 8192],
            in1=Q_sb[:, pp * 8192:(pp + 1) * 8192], op=ALU.mult),
            waits=[prev_red, (sT, sT.n)], sem=sV)
        for j in range(16):
            pm = bank[j % 2]
            t_mm = E_("tensor", lambda e, jj=j, p=pm: e.matmul(
                p[0:1, :], ones_b[:, :], ksq[:, jj * 512:(jj + 1) * 512],
                start=True, stop=True), waits=[t_qsq, prev_red], sem=sT)
            t_red = E_("vector", lambda e, p=pm: e.tensor_reduce(
                out=sc1[:, :], in_=p[0:1, :], axis=AX.X, op=ALU.max),
                waits=[t_mm, prev_red], sem=sV)
            prev_red = E_("vector", lambda e: e.tensor_tensor(
                out=qmax[:, :], in0=qmax[:, :], in1=sc1[:, :], op=ALU.max),
                waits=[t_red], sem=sV)
    t_l1 = E_("scalar", lambda e: e.activation(out=sc1[:, :], in_=kmax[:, :], func=AF.Ln),
              waits=[prev_red], sem=sA)
    t_l2 = E_("scalar", lambda e: e.activation(out=sc2[:, :], in_=qmax[:, :], func=AF.Ln),
              waits=[prev_red], sem=sA)
    t_ad = E_("vector", lambda e: e.tensor_tensor(out=sc1[:, :], in0=sc1[:, :],
              in1=sc2[:, :], op=ALU.add), waits=[t_l1, t_l2], sem=sV)
    t_ex = E_("scalar", lambda e: e.activation(out=sc1[:, :], in_=sc1[:, :],
              func=AF.Exp, scale=0.5), waits=[t_ad], sem=sA)
    t_ng = E_("vector", lambda e: e.tensor_scalar(out=sc1[:, :], in0=sc1[:, :],
              scalar1=-ISQ, scalar2=None, op0=ALU.mult), waits=[t_ex], sem=sV)
    t_bm = E_("tensor", lambda e: e.matmul(bank[2][:, 0:1], ones_f[:, :], sc1[0:1, 0:1],
              start=True, stop=True), waits=[t_ng, t_onf], sem=sT)
    t_bM = E_("vector", lambda e: e.tensor_copy(out=biasM[:, :], in_=bank[2][:, 0:1]),
              waits=[t_bm], sem=sV)

    out_fm = arB[:, :]
    EB2 = [carve(arS, i * 1024, (i + 1) * 1024) for i in range(6)]
    B_rb = carve(arS, 6144, 6656)
    B_acc = carve(arS, 7168, 8192, F32)
    FT01 = carve(arS, 8192, 9216)
    FT2 = carve(arS, 9216, 9728)
    rcrow = arS[0:1, 9728:10752].bitcast(F32)
    PP = [pq01[:, :], pq23[:, :], pq45[:, :]]
    pavb = bank[0]
    psmb = bank[1]
    NPAIR = KCN // 2

    prev_last_av = t_bM
    prev_last_fold = t_bM
    prev_last_exp = t_bM
    prev_accread = t_bM
    prev_rb_read = t_bM
    prev_psm_read = t_bM
    slot_av = {}
    slot_fold = {}
    hq = 0
    for h in range(DBG_BH):
        for q in range(NQC):
            qsl = slice(h * TOK + q * 512, h * TOK + (q + 1) * 512)
            base_war = (sEp, hq) if hq >= 1 else None
            exp_toks = {}
            av_toks = {}
            facc = None
            for kp in range(NPAIR + 2):
                if kp < NPAIR:
                    w_sc = [exp_toks.get(kp - 3)]
                    if kp == 0:
                        w_sc += [tok_aq_V, t_bM]
                    if kp < 3:
                        w_sc.append(prev_last_exp)
                    E_("tensor", lambda e, k=kp, hs=qsl: e.matmul(
                        PP[k % 3][:, 0:512], K_fm[:, 2 * k * 128:(2 * k + 1) * 128],
                        Q_sb[:, hs], start=True, stop=True), waits=w_sc)
                    E_("tensor", lambda e, k=kp, hs=qsl: e.matmul(
                        PP[k % 3][:, 512:1024], K_fm[:, (2 * k + 1) * 128:(2 * k + 2) * 128],
                        Q_sb[:, hs], start=True, stop=True), sem=sT)
                    sc_tok = (sT, sT.n)
                    w_ex = [sc_tok, slot_av.get(kp % 6), slot_fold.get(kp % 6), t_bM]
                    exp_toks[kp] = E_("scalar", lambda e, k=kp: e.activation(
                        out=EB2[k % 6][:], in_=PP[k % 3][:, :], func=AF.Exp,
                        scale=ISQ, bias=biasM[:, 0:1]), waits=w_ex, sem=sA)
                j = kp - 2
                if 0 <= j < NPAIR:
                    E_("tensor", lambda e, k=j: e.matmul(
                        pavb, V_tm[:, 2 * k * 128:(2 * k + 1) * 128],
                        EB2[k % 6][:, 0:512],
                        start=(k == 0), stop=False, skip_group_check=True),
                        waits=[exp_toks[j]] + ([base_war] if j == 0 else []),
                        force=(j == 0))
                    av_toks[j] = E_("tensor", lambda e, k=j: e.matmul(
                        pavb, V_tm[:, (2 * k + 1) * 128:(2 * k + 2) * 128],
                        EB2[k % 6][:, 512:1024],
                        start=False, stop=(k == NPAIR - 1), skip_group_check=True),
                        sem=sT)
                    slot_av[j % 6] = av_toks[j]
                    if j % 2 == 1:
                        t_f0 = E_("vector", lambda e, k=j: e.tensor_tensor(
                            out=FT01[:], in0=EB2[(k - 1) % 6][:, :],
                            in1=EB2[k % 6][:, :], op=ALU.add),
                            waits=[exp_toks[j - 1], exp_toks[j], facc], sem=sV)
                        t_f2 = E_("vector", lambda e: e.tensor_tensor(
                            out=FT2[:], in0=FT01[:, 0:512], in1=FT01[:, 512:1024],
                            op=ALU.add), waits=[t_f0], sem=sV)
                        if j == 1:
                            facc = E_("vector", lambda e: e.tensor_copy(
                                out=B_acc[:], in_=FT2[:]),
                                waits=[t_f2, prev_accread], sem=sV)
                        else:
                            facc = E_("vector", lambda e: e.tensor_tensor(
                                out=B_acc[:], in0=B_acc[:], in1=FT2[:], op=ALU.add),
                                waits=[t_f2], sem=sV)
                        slot_fold[(j - 1) % 6] = facc
                        slot_fold[j % 6] = facc
            E_("vector", lambda e, hs=qsl: e.tensor_copy(
                out=out_fm[:, hs], in_=pavb),
                waits=[av_toks[NPAIR - 1]], sem=sEp)
            t_sum = E_("tensor", lambda e: e.matmul(
                psmb[0:1, :], ones_fc[:, :], B_acc[:], start=True, stop=True),
                waits=[facc, prev_psm_read, t_onfc], sem=sT)
            prev_accread = t_sum
            t_rc = E_("vector", lambda e: e.reciprocal(
                out=rcrow[:, :], in_=psmb[0:1, :]), waits=[t_sum], sem=sV)
            t_bc = E_("tensor", lambda e: e.matmul(
                psmb[:, :], ones_f[:, :], rcrow[0:1, :], start=True, stop=True),
                waits=[t_rc], sem=sT)
            t_rb2 = E_("vector", lambda e: e.tensor_copy(
                out=B_rb[:], in_=psmb[:, :]), waits=[t_bc, prev_rb_read], sem=sV)
            tok_norm = E_("vector", lambda e, hs=qsl: e.tensor_tensor(
                out=out_fm[:, hs], in0=out_fm[:, hs], in1=B_rb[:], op=ALU.mult),
                waits=[t_rb2], sem=sV)
            tok_epilog = (sEp, hq + 1)
            prev_rb_read = tok_norm
            prev_psm_read = t_rb2
            prev_last_av = av_toks[NPAIR - 1]
            prev_last_fold = facc
            prev_last_exp = exp_toks[NPAIR - 1]
            hq += 1
    tok_b_done = (sV, sV.n)

    if dump == "B":
        _dump(EM, y_v, [(out_fm, 0, 32768)], [tok_b_done, (sT, sT.n)], sSt, sDmp)
        _tail(EM, ALL_DMA_SEMS)
        return nc, EM

    woT = [carve(arS, 0, 2048), carve(arS, 2048, 4096)]
    r2b = carve(arS, 4096, 5120, F32)
    m2b = carve(arS, 5120, 6144, F32)
    x1b_t = carve(arS, 6144, 6656)
    x1sq_t = carve(arS, 6656, 7168)
    xt_f = carve(arS, 7168, 8192, F32)
    aff_f = arS[0:16, 8192:9216].bitcast(F32)
    afftm = arS[:, 9216:9248].bitcast(F32)
    tk_m1 = arS[:, 9248:9250].bitcast(F32)
    tk_m2 = arS[:, 9250:9252].bitcast(F32)
    tk_eq1 = arS[:, 9252:9284].bitcast(F32)
    tk_eq2 = arS[:, 9284:9316].bitcast(F32)
    tk_a2 = arS[:, 9316:9348].bitcast(F32)
    tk_d = arS[:, 9348:9350].bitcast(F32)
    tk_t1 = arS[:, 9350:9382].bitcast(F32)
    tk_ctm = arS[:, 9382:9398]
    C_tmp = carve(arS, 13312, 14336, F32)
    C_R1 = arS[0:1, 10240:11264].bitcast(F32)
    C_R2 = arS[0:1, 11264:12288].bitcast(F32)
    C_R3 = arS[0:1, 12288:13312].bitcast(F32)
    x1v = [arC[:, :].bitcast(F32), arD[:, :].bitcast(F32)]

    pC = [bank[0], bank[1]]
    pS2, pQ2 = bank[2], bank[3]
    pR2 = bank[4]
    pAf = bank[5]
    pTr = bank[6]
    h2 = arA[:, :]

    tok_bT = (sT, sT.n)
    tok_woT_war = [[tok_b_done, tok_bT], [tok_b_done, tok_bT]]
    tok_x1_war = tok_bT
    tok_cd_last = None
    tok_d_apply = []
    for q in range(NQC):
        sum_tok = None
        sq_tok = None
        x1toks = {}
        st_toks = {}
        for dc in range(DC):
            idx = dc % 2
            wol = tok_woT_war[idx]
            t_wo = E_("gpsimd", lambda e, d=dc, i=idx: e.dma_start(
                out=woT[i], in_=d_woTt[d]),
                waits=(wol if isinstance(wol, list) else [wol]),
                sem=sWo[idx], dma=True)
            pb = pC[dc % 2]
            tok_mm = None
            for chc in range(NH):
                tok_mm = E_("tensor", lambda e, c=chc, p=pb, i=idx, qq=q: e.matmul(
                    p[:, :], woT[i][:, c * 128:(c + 1) * 128],
                    out_fm[:, c * TOK + qq * 512:c * TOK + (qq + 1) * 512],
                    start=(c == 0), stop=(c == NH - 1)),
                    waits=[t_wo, tok_b_done, x1toks.get(dc - 2)] if chc == 0 else [],
                    sem=(sT if chc == NH - 1 else None))
            tok_woT_war[idx] = tok_mm
            t_xt = E_("sync", lambda e, d=dc, qq=q: e.dma_start(
                out=xt_f, in_=d_xTown.rearrange("(c p) t -> p c t", p=128)
                [:, d, qq * 512:(qq + 1) * 512]), waits=[tok_x1_war], sem=sXt, dma=True)
            x1sl = x1v[dc // 8][:, (dc % 8) * 512:(dc % 8 + 1) * 512]
            t_x1 = E_("vector", lambda e, s=x1sl, p=pb: e.tensor_tensor(
                out=s, in0=p[:, :], in1=xt_f[:], op=ALU.add),
                waits=[tok_mm, t_xt, (sSt, sSt.n)], sem=sV)
            x1toks[dc] = t_x1
            tok_x1_war = t_x1
            st_toks[dc] = E_("sync", lambda e, s=x1sl, d=dc, qq=q: e.dma_start(
                out=d_x1s[d * 128:(d + 1) * 128, qq * 512:(qq + 1) * 512], in_=s),
                waits=[t_x1], sem=sSt, dma=True)
            t_xb = E_("vector", lambda e, s=x1sl: e.tensor_copy(out=x1b_t[:], in_=s),
                      waits=[t_x1, sq_tok], sem=sV)
            t_xq = E_("vector", lambda e: e.tensor_tensor(out=x1sq_t[:], in0=x1b_t[:],
                      in1=x1b_t[:], op=ALU.mult), waits=[t_xb], sem=sV)
            sum_tok = E_("tensor", lambda e, d=dc: e.matmul(
                pS2[0:1, :], ones_b[:, :], x1b_t[:],
                start=(d == 0), stop=(d == DC - 1)), waits=[t_xb], sem=sT)
            sq_tok = E_("tensor", lambda e, d=dc: e.matmul(
                pQ2[0:1, :], ones_b[:, :], x1sq_t[:],
                start=(d == 0), stop=(d == DC - 1)), waits=[t_xq], sem=sT)
        cdT, cdV = (sT, sT.n), (sV, sV.n)
        t_mean = E_("scalar", lambda e: e.mul(C_R1[:, :], pS2[0:1, :], 1.0 / 2048.0),
                    waits=[sum_tok, cdT, cdV], sem=sA)
        t_msq = E_("scalar", lambda e: e.mul(C_R2[:, :], pQ2[0:1, :], 1.0 / 2048.0),
                   waits=[sq_tok, cdT, cdV], sem=sA)
        t_m2 = E_("vector", lambda e: e.tensor_tensor(out=C_R3[:, :], in0=C_R1[:, :],
                  in1=C_R1[:, :], op=ALU.mult), waits=[t_mean], sem=sV)
        t_var = E_("vector", lambda e: e.tensor_tensor(out=C_R2[:, :], in0=C_R2[:, :],
                   in1=C_R3[:, :], op=ALU.subtract), waits=[t_msq, t_m2], sem=sV)
        t_varc = E_("vector", lambda e: e.tensor_scalar(out=C_R2[:, :], in0=C_R2[:, :],
                    scalar1=zero1[0:1, 0:1], scalar2=None, op0=ALU.max),
                    waits=[t_var], sem=sV)
        t_ln = E_("scalar", lambda e: e.activation(out=C_R2[:, :], in_=C_R2[:, :],
                  func=AF.Ln, bias=eps_t[0:1, 0:1]), waits=[t_varc], sem=sA)
        t_r = E_("scalar", lambda e: e.activation(out=C_R2[:, :], in_=C_R2[:, :],
                 func=AF.Exp, scale=-0.5), waits=[t_ln], sem=sA)
        t_mr = E_("vector", lambda e: e.tensor_tensor(out=C_R1[:, :], in0=C_R1[:, :],
                  in1=C_R2[:, :], op=ALU.mult), waits=[t_r], sem=sV)
        t_b1m = E_("tensor", lambda e: e.matmul(pR2[:, :], ones_f[:, :], C_R2[0:1, :],
                   start=True, stop=True), waits=[t_r], sem=sT)
        t_rb_ = E_("vector", lambda e: e.tensor_copy(out=r2b[:], in_=pR2[:, :]),
                   waits=[t_b1m, cdV], sem=sV)
        t_b2m = E_("tensor", lambda e: e.matmul(pR2[:, :], ones_f[:, :], C_R1[0:1, :],
                   start=True, stop=True), waits=[t_mr, t_rb_], sem=sT)
        t_mb_ = E_("vector", lambda e: e.tensor_copy(out=m2b[:], in_=pR2[:, :]),
                   waits=[t_b2m, cdV], sem=sV)
        apply_toks = []
        ap_war = None
        for dc in range(DC):
            x1sl = x1v[dc // 8][:, (dc % 8) * 512:(dc % 8 + 1) * 512]
            h2sl = h2[:, dc * TOK + q * 512:dc * TOK + (q + 1) * 512]
            t_a1 = E_("vector", lambda e, s=x1sl: e.tensor_tensor(
                out=C_tmp, in0=s, in1=r2b[:], op=ALU.mult),
                waits=[t_rb_, ap_war], sem=sV)
            t_ap = E_("vector", lambda e, hh=h2sl: e.tensor_tensor(
                out=hh, in0=C_tmp, in1=m2b[:], op=ALU.subtract),
                waits=[t_a1, t_mb_], sem=sV)
            ap_war = t_ap
            apply_toks.append(t_ap)
        tok_d_apply += apply_toks
        tok_aff = None
        for dinc in range(DC):
            tok_aff = E_("tensor", lambda e, d=dinc, qq=q: e.matmul(
                pAf[0:16, 0:512], wa_sb[:, d * NE:(d + 1) * NE],
                h2[:, d * TOK + qq * 512:d * TOK + (qq + 1) * 512],
                start=(d == 0), stop=(d == DC - 1)),
                waits=[apply_toks[dinc], t_wa], sem=(sT if dinc == DC - 1 else None))
        t_sg = E_("scalar", lambda e: e.activation(out=aff_f[:, :], in_=pAf[0:16, 0:512],
                  func=AF.Sigmoid), waits=[tok_aff], sem=sA)
        prev_ct = None
        for sub in range(4):
            t_tr = E_("tensor", lambda e, s=sub: e.transpose(
                pTr[:, 0:16], aff_f[0:16, s * 128:(s + 1) * 128], idn_f[0:16, 0:16]),
                waits=[t_sg, t_idf, prev_ct], sem=sT)
            t_at = E_("vector", lambda e: e.tensor_copy(out=afftm[:, :], in_=pTr[:, 0:16]),
                      waits=[t_tr, prev_ct], sem=sV)
            t_m1_ = E_("vector", lambda e: e.tensor_reduce(out=tk_m1[:, :], in_=afftm[:, :],
                       axis=AX.X, op=ALU.max), waits=[t_at], sem=sV)
            t_e1 = E_("vector", lambda e: e.tensor_scalar(out=tk_eq1[:, :], in0=afftm[:, :],
                      scalar1=tk_m1[:, 0:1], scalar2=None, op0=ALU.is_equal),
                      waits=[t_m1_], sem=sV)
            t_a2_ = E_("vector", lambda e: e.scalar_tensor_tensor(
                out=tk_a2[:, :], in0=tk_eq1[:, :], scalar=n1e9[:, 0:1], in1=afftm[:, :],
                op0=ALU.mult, op1=ALU.add), waits=[t_e1, t_n9], sem=sV)
            t_m2_ = E_("vector", lambda e: e.tensor_reduce(out=tk_m2[:, :], in_=tk_a2[:, :],
                       axis=AX.X, op=ALU.max), waits=[t_a2_], sem=sV)
            t_e2 = E_("vector", lambda e: e.tensor_scalar(out=tk_eq2[:, :], in0=tk_a2[:, :],
                      scalar1=tk_m2[:, 0:1], scalar2=None, op0=ALU.is_equal),
                      waits=[t_m2_], sem=sV)
            t_dn = E_("vector", lambda e: e.tensor_tensor(out=tk_d[:, :], in0=tk_m1[:, :],
                      in1=tk_m2[:, :], op=ALU.add), waits=[t_m2_], sem=sV)
            t_rd = E_("vector", lambda e: e.reciprocal(out=tk_d[:, :], in_=tk_d[:, :]),
                      waits=[t_dn], sem=sV)
            t_p1 = E_("vector", lambda e: e.tensor_tensor(out=tk_m1[:, :], in0=tk_m1[:, :],
                      in1=tk_d[:, :], op=ALU.mult), waits=[t_rd], sem=sV)
            t_p2 = E_("vector", lambda e: e.tensor_tensor(out=tk_m2[:, :], in0=tk_m2[:, :],
                      in1=tk_d[:, :], op=ALU.mult), waits=[t_rd], sem=sV)
            t_t1 = E_("vector", lambda e: e.tensor_scalar(out=tk_t1[:, :], in0=tk_eq1[:, :],
                      scalar1=tk_m1[:, 0:1], scalar2=None, op0=ALU.mult),
                      waits=[t_p1, t_e2], sem=sV)
            t_t2 = E_("vector", lambda e: e.tensor_scalar(out=tk_eq2[:, :], in0=tk_eq2[:, :],
                      scalar1=tk_m2[:, 0:1], scalar2=None, op0=ALU.mult),
                      waits=[t_p2], sem=sV)
            t_ct = E_("vector", lambda e: e.tensor_tensor(out=tk_ctm[:, 0:16], in0=tk_t1[:, :],
                      in1=tk_eq2[:, :], op=ALU.add), waits=[t_t1, t_t2], sem=sV)
            t_tb = E_("tensor", lambda e: e.transpose(
                pTr[0:16, :].bitcast(BF16)[0:16, 0:128], tk_ctm[:, 0:16], idn_b[:, :]),
                waits=[t_ct], sem=sT)
            prev_ct = E_("vector", lambda e, qq=q, s=sub: e.tensor_copy(
                out=c_fm[0:16, qq * 512 + s * 128:qq * 512 + (s + 1) * 128],
                in_=pTr[0:16, :].bitcast(BF16)[0:16, 0:128]),
                waits=[t_tb], sem=sV)
        tok_cd_last = prev_ct
    tok_d_done_v = (sV, sV.n)
    tok_d_done_t = (sT, sT.n)

    if dump == "CD":
        _dump(EM, y_v, [(h2[:, 0:16384], 0, 16384)],
              [tok_d_done_v, tok_d_done_t], sSt, sDmp,
              extra=[(c_fm[0:16, :], 16384, 2048)])
        _tail(EM, ALL_DMA_SEMS)
        return nc, EM

    accv = arB[:, :].bitcast(F32)
    y3ar = [arC[:, :], arD[:, :]]
    gblk = [carve(arS, 0, 2048), carve(arS, 2048, 4096)]
    w1blk = [carve(arS, 4096, 6144), carve(arS, 6144, 8192)]
    w2blk = [carve(arS, 8192, 10240), carve(arS, 10240, 12288)]
    ygt = [carve(arS, 12288, 12800), carve(arS, 12800, 13312)]
    cbt = [carve(arS, 13312, 13824), carve(arS, 13824, 14336)]
    c_row = arS[0:1, 14336:16384].bitcast(F32)
    x1f = [carve(arS, 8192, 10240, F32), carve(arS, 10240, 12288, F32)]

    pG = [bank[0], bank[1]]
    pL = [bank[2], bank[3]]
    pO = [bank[4], bank[6]]
    pCB = bank[5]

    tok_g_war = [tok_d_done_t, tok_d_done_t]
    tok_w1_war = [tok_d_done_t, tok_d_done_t]
    tok_w2_war = [tok_d_done_t, tok_d_done_t]
    tok_cd_st = (sSt, sSt.n)
    tok_y3_war = [[tok_d_done_t, tok_cd_st], [tok_d_done_t, tok_cd_st]]
    tok_pg_war = {}
    tok_po_war = {}
    tok_cbt_read = [t_bM, t_bM]
    tok_cb_pcb = t_bM
    tok_rb2_war = None
    for tc2 in range(2):
        tg = tc2 * 1024
        acc_init = [False] * (DC * 2)
        last_out_tok = None
        for ep in range(NEX // 2):
            for el in range(2):
                e_idx = 2 * ep + el
                cb_toks = [None, None]
                if e_idx < NE:
                    t_crow = E_("sync", lambda e, ee=e_idx, t0=tg: e.dma_start(
                        out=c_row[:, :], in_=c_fm[ee:ee + 1, t0:t0 + 1024]),
                        waits=[tok_cd_last, tok_cb_pcb], sem=sCrow, dma=True)
                    for th in range(2):
                        t_cm = E_("tensor", lambda e, h=th: e.matmul(
                            pCB[:, :], ones_f[:, :],
                            c_row[0:1, h * 512:(h + 1) * 512],
                            start=True, stop=True),
                            waits=[t_crow, tok_cb_pcb, tok_cbt_read[th]], sem=sT)
                        cb_toks[th] = E_("vector", lambda e, h=th: e.tensor_copy(
                            out=cbt[h][:], in_=pCB[:, :]), waits=[t_cm], sem=sV)
                        tok_cb_pcb = cb_toks[th]
                for fc in range(FC):
                    i = fc % 2
                    t_g = E_("gpsimd", lambda e, ee=e_idx, f=fc, i2=i: e.dma_start(
                        out=gblk[i2], in_=d_g[ee, f]), waits=[tok_g_war[i]],
                        sem=sG[i], dma=True)
                    t_w = E_("gpsimd", lambda e, ee=e_idx, f=fc, i2=i: e.dma_start(
                        out=w1blk[i2], in_=d_w1[ee, f]), waits=[tok_w1_war[i]],
                        sem=sW1[i], dma=True)
                    for th in range(2):
                        t0 = tg + th * 512
                        tokg = None
                        for dc in range(DC):
                            tokg = E_("tensor", lambda e, d=dc, i2=i, h=th, s=t0: e.matmul(
                                pG[h][:, :], gblk[i2][:, d * 128:(d + 1) * 128],
                                h2[:, d * TOK + s:d * TOK + s + 512],
                                start=(d == 0), stop=(d == DC - 1)),
                                waits=[t_g, tok_d_done_v,
                                       tok_pg_war.get(('g', th))] if dc == 0 else [],
                                sem=(sT if dc == DC - 1 else None))
                        tokl = None
                        for dc in range(DC):
                            tokl = E_("tensor", lambda e, d=dc, i2=i, h=th, s=t0: e.matmul(
                                pL[h][:, :], w1blk[i2][:, d * 128:(d + 1) * 128],
                                h2[:, d * TOK + s:d * TOK + s + 512],
                                start=(d == 0), stop=(d == DC - 1)),
                                waits=[t_w, tok_pg_war.get(('l', th))] if dc == 0 else [],
                                sem=(sT if dc == DC - 1 else None))
                        bcol = e_idx * FC + fc
                        t_yg = E_("scalar", lambda e, h=th, bc=bcol: e.activation(
                            out=ygt[h][:], in_=pG[h][:, :], func=AF.Gelu,
                            bias=gb_sb[:, bc:bc + 1]),
                            waits=[tokg, t_gb, tok_pg_war.get(('yg', th))], sem=sA)
                        tok_pg_war[('g', th)] = t_yg
                        y3sl = y3ar[el][:, fc * 1024 + th * 512:fc * 1024 + (th + 1) * 512]
                        t_y3 = E_("vector", lambda e, h=th, bc=bcol, s=y3sl: e.scalar_tensor_tensor(
                            out=s, in0=pL[h][:, :], scalar=b1_sb[:, bc:bc + 1],
                            in1=ygt[h][:], op0=ALU.add, op1=ALU.mult),
                            waits=[tokl, t_yg, t_b1c] + tok_y3_war[el], sem=sV)
                        tok_pg_war[('l', th)] = t_y3
                        tok_pg_war[('yg', th)] = t_y3
                        if e_idx < NE:
                            t_y3 = E_("vector", lambda e, h=th, s=y3sl: e.tensor_tensor(
                                out=s, in0=s, in1=cbt[h][:], op=ALU.mult),
                                waits=[t_y3, cb_toks[th]], sem=sV)
                            tok_cbt_read[th] = t_y3
                    tok_g_war[i] = (sT, sT.n)
                    tok_w1_war[i] = (sT, sT.n)
            tok_s1_done = (sV, sV.n)
            for dc in range(DC):
                i = dc % 2
                t_w2 = E_("gpsimd", lambda e, p=ep, d=dc, i2=i: e.dma_start(
                    out=w2blk[i2], in_=d_w2[p, d]), waits=[tok_w2_war[i]],
                    sem=sW2[i], dma=True)
                if ep == 0:
                    t_r2l = E_("sync", lambda e, d=dc: e.dma_start(
                        out=rb2_sb[:, :], in_=d_rb2[0:16, d * 128:(d + 1) * 128]),
                        waits=[tok_rb2_war], sem=sRb2, dma=True)
                    t_r2l2 = E_("sync", lambda e, d=dc: e.dma_start(
                        out=sb2_sb[:, :], in_=d_rb2[32:34, d * 128:(d + 1) * 128]),
                        waits=[tok_rb2_war], sem=sRb2, dma=True)
                for th in range(2):
                    nmm = 2 * FC + (2 if ep == 0 else 0)
                    k = 0
                    tok_o = None
                    for el in range(2):
                        for fc in range(FC):
                            first = (k == 0)
                            last = (k == nmm - 1)
                            w0 = [t_w2, tok_s1_done, tok_po_war.get(th),
                                  last_out_tok] if first else []
                            tok_o = E_("tensor", lambda e, l=el, f=fc, h=th, i2=i,
                                       st=first, sp=last: e.matmul(
                                pO[h][:, :],
                                w2blk[i2][:, (l * FC + f) * 128:(l * FC + f + 1) * 128],
                                y3ar[l][:, f * 1024 + h * 512:f * 1024 + (h + 1) * 512],
                                start=st, stop=sp, skip_group_check=True),
                                waits=w0, sem=(sT if last else None))
                            k += 1
                    if ep == 0:
                        tok_o = E_("tensor", lambda e, h=th, t0=tg: e.matmul(
                            pO[h][:, :], rb2_sb[:, :],
                            c_fm[0:16, t0 + h * 512:t0 + (h + 1) * 512],
                            start=False, stop=False, skip_group_check=True),
                            waits=[t_r2l2, tok_cd_last])
                        k += 1
                        tok_o = E_("tensor", lambda e, h=th: e.matmul(
                            pO[h][:, :], sb2_sb[:, :], ones2f[:, :],
                            start=False, stop=True, skip_group_check=True),
                            waits=[t_on2, t_r2l2], sem=sT)
                    accsl = accv[:, dc * 1024 + th * 512:dc * 1024 + (th + 1) * 512]
                    if not acc_init[dc * 2 + th]:
                        t_ac = E_("vector", lambda e, a=accsl, h=th: e.tensor_copy(
                            out=a, in_=pO[h][:, :]),
                            waits=[tok_o, tok_b_done,
                                   tok_po_war.get(('prev_tc', dc, th))], sem=sV)
                        acc_init[dc * 2 + th] = True
                    else:
                        t_ac = E_("vector", lambda e, a=accsl, h=th: e.tensor_tensor(
                            out=a, in0=a, in1=pO[h][:, :], op=ALU.add),
                            waits=[tok_o], sem=sV)
                    tok_po_war[th] = t_ac
                tok_w2_war[i] = (sT, sT.n)
                if ep == 0:
                    tok_rb2_war = (sT, sT.n)
            tok_y3_war = [[(sT, sT.n)], [(sT, sT.n)]]
        prev_x1 = [tok_g_war[0], tok_g_war[1]]
        fin_toks = {}
        for dc in range(DC):
            i = dc % 2
            t_xl = E_("sync", lambda e, d=dc, t0=tg, i2=i: e.dma_start(
                out=x1f[i2], in_=d_x1s[d * 128:(d + 1) * 128, t0:t0 + 1024]),
                waits=[prev_x1[i], (sT, sT.n) if dc < 2 else None,
                       (sSt, sSt.n) if dc < 2 else None], sem=sX1f[i], dma=True)
            accsl = accv[:, dc * 1024:(dc + 1) * 1024]
            t_fh = E_("vector", lambda e, a=accsl, d=dc, t0=tg: e.tensor_tensor(
                out=a, in0=a, in1=h2[:, d * TOK + t0:d * TOK + t0 + 1024],
                op=ALU.add),
                waits=[tok_po_war.get(0), tok_po_war.get(1)], sem=sV)
            t_fin = E_("vector", lambda e, a=accsl, i2=i: e.tensor_tensor(
                out=a, in0=a, in1=x1f[i2][:], op=ALU.add),
                waits=[t_xl, t_fh], sem=sV)
            prev_x1[i] = t_fin
            tok_st = E_("sync", lambda e, a=accsl, d=dc, t0=tg: e.dma_start(
                out=d_y[d * 128:(d + 1) * 128, t0:t0 + 1024], in_=a),
                waits=[t_fin], sem=sSt, dma=True)
            fin_toks[dc] = tok_st
            last_out_tok = tok_st
        for dc in range(DC):
            for th in range(2):
                tok_po_war[('prev_tc', dc, th)] = fin_toks[DC - 1]
        tok_g_war = [(sT, sT.n), (sT, sT.n)]
        tok_w1_war = [tok_g_war[0], tok_g_war[1]]
        tok_w2_war = [prev_x1[0], prev_x1[1]]
    _tail(EM, ALL_DMA_SEMS)
    return nc, EM


def _tail(EM, sems):
    for sm in sems:
        EM.step("sync", lambda e, s=sm: e.wait_ge(s.h, s.n), [])


def _dump(EM, y_v, items, waits, sSt, sDmp, extra=(), fitems=()):
    for ap, ci in fitems:
        n = ap.shape[-1]
        EM.step("sync", lambda e, a=ap, c=ci, nn=n: e.dma_start(
            out=y_v[:, c, 0:nn], in_=a), list(waits), sSt, True)
    for ap, col, n in items:
        c0, nch = col // 2048, n // 2048
        EM.step("gpsimd", lambda e, a=ap, c=c0, k=nch: e.dma_start(
            out=y_v[:, c:c + k, :],
            in_=a.rearrange("p (c t) -> p c t", c=k)), list(waits), sDmp, True)
    for ap, col, n in extra:
        c0, nch = col // 2048, n // 2048
        EM.step("gpsimd", lambda e, a=ap, c=c0, k=nch: e.dma_start(
            out=y_v[0:16, c:c + k, :],
            in_=a.rearrange("p (c t) -> p c t", c=k)), list(waits), sDmp, True)



def finalize(nc, EM):
    with nc.Block() as block:
        for name in ENGS:
            steps = EM.steps[name]

            def mk(steps):
                def body(eng):
                    for fn, ws, sem, amt in steps:
                        for (s, v) in ws:
                            eng.wait_ge(s.h, v)
                        ins = fn(eng)
                        if sem is not None and ins is not None:
                            ins.then_inc(sem.h, amt)
                return body
            getattr(block, name)(mk(steps))
    return nc



def _host_prep(inputs):
    x = np.asarray(inputs["L0_attn_x"], dtype=np.float32)
    w_attn = np.asarray(inputs["L0_attn_w_attn"], dtype=np.float32)
    w_out = np.asarray(inputs["L0_attn_w_out"], dtype=np.float32)
    wa = np.asarray(inputs["L0_ffn_wa"], dtype=np.float32)
    rg = np.asarray(inputs["L0_ffn_rg"], dtype=np.float32)
    rgb = np.asarray(inputs["L0_ffn_rgb"], dtype=np.float32)
    rw1 = np.asarray(inputs["L0_ffn_rw1"], dtype=np.float32)
    rb1 = np.asarray(inputs["L0_ffn_rb1"], dtype=np.float32)
    rw2 = np.asarray(inputs["L0_ffn_rw2"], dtype=np.float32)
    rb2 = np.asarray(inputs["L0_ffn_rb2"], dtype=np.float32)
    sg = np.asarray(inputs["L0_ffn_sg"], dtype=np.float32)
    sgb = np.asarray(inputs["L0_ffn_sgb"], dtype=np.float32)
    sw1 = np.asarray(inputs["L0_ffn_sw1"], dtype=np.float32)
    sb1 = np.asarray(inputs["L0_ffn_sb1"], dtype=np.float32)
    sw2 = np.asarray(inputs["L0_ffn_sw2"], dtype=np.float32)
    sb2 = np.asarray(inputs["L0_ffn_sb2"], dtype=np.float32)

    c = np.ascontiguousarray

    pos = np.asarray(10000.0 ** ((-2.0 * np.arange(0, HD, 2) - 1.0) / HD),
                     dtype=np.float32)
    angles = np.arange(T, dtype=np.float32)[:, None] * pos[None, :]
    rot = np.concatenate((angles, angles), axis=-1)
    cosT = c(np.cos(rot).astype(np.float32).T)
    sinT = c(np.sin(rot).astype(np.float32).T)

    waT = w_attn.T
    waTt = waT.reshape(DC, 128, NH + 2, 128).transpose(1, 0, 2, 3)
    waTkv = c(waTt[:, :, NH:, :].reshape(128, DC * 2 * 128))
    waTq = c(waTt[:, :, :NH, :].transpose(0, 2, 1, 3).reshape(128, NH * DC * 128))

    woT = w_out.T
    woTt = c(woT.reshape(NH, 128, DC, 128).transpose(2, 1, 0, 3).reshape(DC, 128, D))

    wa_t = c(wa.reshape(DC, 128, NE).transpose(1, 0, 2).reshape(128, DC * NE))

    A_g = np.concatenate([rg, sg], axis=0)
    A_w1 = np.concatenate([rw1, sw1], axis=0)
    A_w2 = np.concatenate([rw2, sw2], axis=0)
    g_t = c(A_g.reshape(NEX, DC, 128, FC, 128).transpose(0, 3, 2, 1, 4)
            .reshape(NEX, FC, 128, DC * 128))
    w1_t = c(A_w1.reshape(NEX, DC, 128, FC, 128).transpose(0, 3, 2, 1, 4)
             .reshape(NEX, FC, 128, DC * 128))
    w2t = A_w2.reshape(NEX, FC, 128, DC, 128).transpose(0, 3, 2, 1, 4)
    w2p = c(w2t.reshape(NEX // 2, 2, DC, 128, FC * 128)
            .transpose(0, 2, 3, 1, 4).reshape(NEX // 2, DC, 128, 2048))

    gball = np.concatenate([rgb, sgb], axis=0)
    b1all = np.concatenate([rb1, sb1], axis=0)
    gb_t = c(gball.reshape(NEX, FC, 128).transpose(2, 0, 1).reshape(128, NEX * FC))
    b1_t = c(b1all.reshape(NEX, FC, 128).transpose(2, 0, 1).reshape(128, NEX * FC))

    rb2sb2 = np.zeros((34, D), dtype=np.float32)
    rb2sb2[0:16] = rb2
    rb2sb2[32:34] = sb2

    idn = np.eye(128, dtype=np.float32)

    shared = dict(cosT=cosT, sinT=sinT, waTkv=waTkv, waTq=waTq, woTt=woTt,
                  wa=wa_t, g_t=g_t, w1_t=w1_t, w2p_t=w2p, gb_t=gb_t, b1_t=b1_t,
                  rb2sb2=rb2sb2, idn=idn)

    xTb = [c(x[b].T) for b in range(B)]
    in_maps = []
    for core in range(8):
        b = core // 4
        q0 = (core % 4) * TOK
        m = dict(shared)
        m["xT"] = xTb[b]
        m["xTown"] = c(xTb[b][:, q0:q0 + TOK])
        m["cosO"] = c(cosT[:, q0:q0 + TOK])
        m["sinO"] = c(sinT[:, q0:q0 + TOK])
        in_maps.append(m)
    return in_maps


def kernel(**inputs):
    in_maps = _host_prep(inputs)
    nc, EM = build_program(dump=None)
    nc = finalize(nc, EM)
    res = run_bass_kernel_spmd(nc, in_maps, core_ids=list(range(8)))
    out = np.empty((B, T, D), dtype=np.float32)
    for core in range(8):
        b = core // 4
        q0 = (core % 4) * TOK
        out[b, q0:q0 + TOK, :] = res.results[core]["y"].T
    return out


# version 07
# speedup vs baseline: 1.1184x; 1.0136x over previous
import numpy as np
import concourse.bass as bass
import concourse.mybir as mybir
from concourse.bass_utils import run_bass_kernel_spmd

F32 = mybir.dt.float32
BF16 = mybir.dt.bfloat16
AF = mybir.ActivationFunctionType
ALU = mybir.AluOpType
AX = mybir.AxisListType

B, T, D = 2, 8192, 2048
NH, HD = 16, 128
NE, NSH, FF = 16, 2, 1024
NEX = NE + NSH
TOK = 2048
DC = D // 128
FC = FF // 128
KCN = T // 128
NTC = T // 512
NQC = TOK // 512
EPS = 1e-5
ISQ = float(1.0 / np.sqrt(128.0))

ENGS = ("sync", "gpsimd", "tensor", "scalar", "vector")


class Sem:
    __slots__ = ("h", "n", "name")

    def __init__(self, nc, name):
        self.h = nc.alloc_semaphore(name)
        self.n = 0
        self.name = name


class Emit:
    def __init__(self):
        self.steps = {e: [] for e in ENGS}
        self.hwm = {e: {} for e in ENGS}

    def step(self, eng, fn, waits=(), sem=None, dma=False, force=False):
        ws = []
        for w in waits:
            if w is None:
                continue
            s, v = w
            if force or self.hwm[eng].get(s, -1) < v:
                if self.hwm[eng].get(s, -1) < v:
                    self.hwm[eng][s] = v
                ws.append((s, v))
        tokn = None
        amt = 0
        if sem is not None:
            amt = 16 if dma else 1
            sem.n += amt
            tokn = (sem, sem.n)
        self.steps[eng].append((fn, ws, sem, amt))
        return tokn


DBG_NTC = NTC
DBG_AQ = 2
DBG_BH = NH


def build_program(dump=None):
    nc = bass.Bass("TRN2", target_bir_lowering=False, debug=False)

    d_xT = nc.dram_tensor("xT", [D, T], F32, kind="ExternalInput").ap()
    d_xTown = nc.dram_tensor("xTown", [D, TOK], F32, kind="ExternalInput").ap()
    d_cosT = nc.dram_tensor("cosT", [HD, T], F32, kind="ExternalInput").ap()
    d_sinT = nc.dram_tensor("sinT", [HD, T], F32, kind="ExternalInput").ap()
    d_cosO = nc.dram_tensor("cosO", [HD, TOK], F32, kind="ExternalInput").ap()
    d_sinO = nc.dram_tensor("sinO", [HD, TOK], F32, kind="ExternalInput").ap()
    d_waTkv = nc.dram_tensor("waTkv", [128, DC * 2 * 128], F32, kind="ExternalInput").ap()
    d_waTq = nc.dram_tensor("waTq", [128, DC * NH * 128], F32, kind="ExternalInput").ap()
    d_woTt = nc.dram_tensor("woTt", [DC, 128, D], F32, kind="ExternalInput").ap()
    d_wa = nc.dram_tensor("wa", [128, DC * NE], F32, kind="ExternalInput").ap()
    d_g = nc.dram_tensor("g_t", [NEX, FC, 128, DC * 128], F32, kind="ExternalInput").ap()
    d_w1 = nc.dram_tensor("w1_t", [NEX, FC, 128, DC * 128], F32, kind="ExternalInput").ap()
    d_w2 = nc.dram_tensor("w2p_t", [NEX // 2, DC, 128, 2048], F32, kind="ExternalInput").ap()
    d_gb = nc.dram_tensor("gb_t", [128, NEX * FC], F32, kind="ExternalInput").ap()
    d_b1 = nc.dram_tensor("b1_t", [128, NEX * FC], F32, kind="ExternalInput").ap()
    d_rb2 = nc.dram_tensor("rb2sb2", [34, D], F32, kind="ExternalInput").ap()
    d_idn = nc.dram_tensor("idn", [128, 128], F32, kind="ExternalInput").ap()
    d_y = nc.dram_tensor("y", [D, TOK], F32, kind="ExternalOutput").ap()
    d_x1s = nc.dram_tensor("x1s", [D, TOK], F32, kind="Internal").ap()
    y_v = d_y.rearrange("(c p) t -> p c t", p=128)

    arA = nc.alloc_sbuf_tensor("arA", [128, 32768], BF16)
    arB = nc.alloc_sbuf_tensor("arB", [128, 32768], BF16)
    arC = nc.alloc_sbuf_tensor("arC", [128, 8192], BF16)
    arD = nc.alloc_sbuf_tensor("arD", [128, 8192], BF16)
    arS = nc.alloc_sbuf_tensor("arS", [128, 16384], BF16)

    ones_f = nc.alloc_sbuf_tensor("ones_f", [1, 128], F32)
    ones_fc = nc.alloc_sbuf_tensor("ones_fc", [128, 1], F32)
    ones_b = nc.alloc_sbuf_tensor("ones_b", [128, 1], BF16)
    ones2f = nc.alloc_sbuf_tensor("ones2f", [2, 512], F32)
    idn_f = nc.alloc_sbuf_tensor("idn_f", [128, 128], F32)
    idn_b = nc.alloc_sbuf_tensor("idn_b", [128, 128], BF16)
    eps_t = nc.alloc_sbuf_tensor("eps_t", [1, 1], F32)
    zero1 = nc.alloc_sbuf_tensor("zero1", [1, 1], F32)
    n1e9 = nc.alloc_sbuf_tensor("n1e9", [128, 1], F32)
    gb_sb = nc.alloc_sbuf_tensor("gb_sb", [128, NEX * FC], F32)
    b1_sb = nc.alloc_sbuf_tensor("b1_sb", [128, NEX * FC], F32)
    wa_sb = nc.alloc_sbuf_tensor("wa_sb", [128, DC * NE], BF16)
    biasM = nc.alloc_sbuf_tensor("biasM", [128, 1], F32)
    c_fm = nc.alloc_sbuf_tensor("c_fm", [16, TOK], F32)
    rb2_sb = nc.alloc_sbuf_tensor("rb2_sb", [16, 128], F32)
    sb2_sb = nc.alloc_sbuf_tensor("sb2_sb", [2, 128], F32)
    kmax = nc.alloc_sbuf_tensor("kmax", [1, 1], F32)
    qmax = nc.alloc_sbuf_tensor("qmax", [1, 1], F32)
    sc1 = nc.alloc_sbuf_tensor("sc1", [1, 1], F32)
    sc2 = nc.alloc_sbuf_tensor("sc2", [1, 1], F32)

    def carve(arena, c0, c1, dt=None):
        v = arena[:, c0:c1]
        if dt is F32:
            v = v.bitcast(F32)
        return v

    pq01 = nc.alloc_psum_tensor("pq01", [128, 1024], F32)
    pq23 = nc.alloc_psum_tensor("pq23", [128, 1024], F32)
    pq45 = nc.alloc_psum_tensor("pq45", [128, 1024], F32)
    ps6 = nc.alloc_psum_tensor("ps6", [128, 512], F32)
    ps7 = nc.alloc_psum_tensor("ps7", [128, 512], F32)
    bank = [ps6[:, :], ps7[:, :], pq45[:, 0:512], pq45[:, 512:1024],
            pq01[:, 0:512], pq01[:, 512:1024],
            pq23[:, 0:512], pq23[:, 512:1024]]

    sSt = Sem(nc, "sSt")
    sT = Sem(nc, "sT")
    sA = Sem(nc, "sA")
    sV = Sem(nc, "sV")
    sCst = Sem(nc, "sCst")
    sCsS = Sem(nc, "sCsS")
    sWq = Sem(nc, "sWq")
    sCos = Sem(nc, "sCos")
    sSin = Sem(nc, "sSin")
    sXt = Sem(nc, "sXt")
    sRb2 = Sem(nc, "sRb2")
    sCrow = Sem(nc, "sCrow")
    sXb = [Sem(nc, "sXb0"), Sem(nc, "sXb1")]
    sWo = [Sem(nc, "sWo0"), Sem(nc, "sWo1")]
    sG = [Sem(nc, "sG0"), Sem(nc, "sG1")]
    sW1 = [Sem(nc, "sW10"), Sem(nc, "sW11")]
    sW2 = [Sem(nc, "sW20"), Sem(nc, "sW21")]
    sX1f = [Sem(nc, "sX1f0"), Sem(nc, "sX1f1")]
    sEp = Sem(nc, "sEp")
    sDmp = Sem(nc, "sDmp")
    ALL_DMA_SEMS = ([sSt, sDmp, sCst, sCsS, sWq, sCos, sSin, sXt, sRb2, sCrow]
                    + sXb + sWo + sG + sW1 + sW2 + sX1f)
    EM = Emit()

    def E_(eng, fn, waits=(), sem=None, dma=False, force=False):
        return EM.step(eng, fn, waits, sem, dma, force)

    E_("sync", lambda e: e.dma_start(out=gb_sb[:], in_=d_gb[:, :]), sem=sCsS, dma=True)
    E_("sync", lambda e: e.dma_start(out=b1_sb[:], in_=d_b1[:, :]), sem=sCsS, dma=True)
    E_("sync", lambda e: e.dma_start(out=idn_f[:], in_=d_idn[:, :]), sem=sCsS, dma=True)
    E_("gpsimd", lambda e: e.dma_start(out=idn_b[:], in_=d_idn[:, :]), sem=sCst, dma=True)
    E_("gpsimd", lambda e: e.dma_start(out=wa_sb[:], in_=d_wa[:, :]), sem=sCst, dma=True)
    E_("gpsimd", lambda e: e.dma_start(out=carve(arS, 0, 4096), in_=d_waTkv[:, :]), sem=sCst, dma=True)
    t_gb = t_b1c = t_idf = (sCsS, sCsS.n)
    t_idb = t_wa = t_wkv = (sCst, sCst.n)
    t_onf = E_("vector", lambda e: e.memset(ones_f[:], 1.0), sem=sV)
    t_onfc = E_("vector", lambda e: e.memset(ones_fc[:], 1.0), sem=sV)
    t_onb = E_("vector", lambda e: e.memset(ones_b[:], 1.0), sem=sV)
    t_on2 = E_("vector", lambda e: e.memset(ones2f[:], 1.0), sem=sV)
    t_eps = E_("vector", lambda e: e.memset(eps_t[:], EPS), sem=sV)
    t_z1 = E_("vector", lambda e: e.memset(zero1[:], 0.0), sem=sV)
    t_n9 = E_("vector", lambda e: e.memset(n1e9[:], -1.0e9), sem=sV)

    S_t1 = carve(arS, 4096, 5120, F32)
    S_t2 = carve(arS, 5120, 6144, F32)
    S_rb = carve(arS, 6144, 7168, F32)
    S_cosr = carve(arS, 7168, 8192, F32)
    S_sinr = carve(arS, 8192, 9216, F32)
    S_vtmp = carve(arS, 9216, 9728)
    S_xsq = [carve(arS, 9728 + 512 * i, 9728 + 512 * (i + 1)) for i in range(2)]
    S_R3 = arS[0:1, 10752:11776].bitcast(F32)
    S_cosc = carve(arS, 11776, 12800, F32)
    S_sinc = carve(arS, 12800, 13824, F32)
    S_wsum = arS[0:1, 13824:14336].bitcast(F32)
    S_R1 = arS[0:1, 14336:15360].bitcast(F32)
    S_R2 = arS[0:1, 15360:16384].bitcast(F32)
    Q_wsum = arS[0:1, 0:4096].bitcast(F32)

    xb_bufs = [carve(arB, 0, 8192), carve(arB, 8192, 16384)]
    waTq_sb = carve(arB, 16384, 32768)
    K_fm = arC[:, :]
    V_tm = arD[:, :]
    Q_sb = arA[:, :]

    ST = {
        "stats_war": None, "sq_war": {}, "rb_war": None, "cos_war": None,
        "qkv_war": {}, "qkv_rot": 0, "v_war": None, "tr_war": None,
    }

    def a_chunk(xb, x_dram_view, wsum_row, dout_list, lhsT_for, evac,
                tok_prev_use, cosd, sind, xbsem):
        pS, pQ = bank[2], bank[3]
        t_ld = E_("gpsimd", lambda e, o=xb, i=x_dram_view: e.dma_start(
            out=o.rearrange("p (c t) -> p c t", c=16), in_=i),
            waits=[tok_prev_use], sem=xbsem, dma=True)
        t_cos = E_("sync", lambda e, i=cosd: e.dma_start(out=S_cosc, in_=i),
                   waits=[ST["cos_war"]], sem=sCos, dma=True)
        t_sin = E_("sync", lambda e, i=sind: e.dma_start(out=S_sinc, in_=i),
                   waits=[ST["cos_war"]], sem=sSin, dma=True)
        tok_sum = None
        for dinc in range(DC):
            tok_sum = E_("tensor", lambda e, d=dinc, x=xb: e.matmul(
                pS[0:1, :], ones_b[:, :], x[:, d * 512:(d + 1) * 512],
                start=(d == 0), stop=(d == DC - 1)),
                waits=[t_ld, ST["stats_war"], t_onb],
                sem=(sT if dinc == DC - 1 else None))
        for dinc in range(DC):
            t_sq = E_("vector", lambda e, d=dinc, x=xb: e.tensor_tensor(
                out=S_xsq[d % 2][:], in0=x[:, d * 512:(d + 1) * 512],
                in1=x[:, d * 512:(d + 1) * 512], op=ALU.mult),
                waits=[t_ld, ST["sq_war"].get(dinc % 2)], sem=sV)
            ST["sq_war"][dinc % 2] = E_("tensor", lambda e, d=dinc: e.matmul(
                pQ[0:1, :], ones_b[:, :], S_xsq[d % 2][:],
                start=(d == 0), stop=(d == DC - 1)),
                waits=[t_sq, ST["stats_war"]], sem=sT)
        tok_sq = ST["sq_war"][(DC - 1) % 2]
        pcT, pcV = ST.get("pcT"), ST.get("pcV")
        t_mean = E_("scalar", lambda e: e.mul(S_R1[:, :], pS[0:1, :], 1.0 / 2048.0),
                    waits=[tok_sum, pcT, pcV], sem=sA)
        t_msq = E_("scalar", lambda e: e.mul(S_R2[:, :], pQ[0:1, :], 1.0 / 2048.0),
                   waits=[tok_sq, pcT, pcV], sem=sA)
        ST["stats_war"] = t_msq
        t_m2 = E_("vector", lambda e: e.tensor_tensor(out=S_R3[:, :], in0=S_R1[:, :],
                  in1=S_R1[:, :], op=ALU.mult), waits=[t_mean, pcV], sem=sV)
        t_var = E_("vector", lambda e: e.tensor_tensor(out=S_R2[:, :], in0=S_R2[:, :],
                   in1=S_R3[:, :], op=ALU.subtract), waits=[t_msq, t_m2], sem=sV)
        t_varc = E_("vector", lambda e: e.tensor_scalar(out=S_R2[:, :], in0=S_R2[:, :],
                    scalar1=zero1[0:1, 0:1], scalar2=None, op0=ALU.max),
                    waits=[t_var, t_z1], sem=sV)
        t_ln = E_("scalar", lambda e: e.activation(out=S_R2[:, :], in_=S_R2[:, :],
                  func=AF.Ln, bias=eps_t[0:1, 0:1]), waits=[t_varc, t_eps], sem=sA)
        t_r = E_("scalar", lambda e: e.activation(out=S_R2[:, :], in_=S_R2[:, :],
                 func=AF.Exp, scale=-0.5), waits=[t_ln], sem=sA)
        t_mr = E_("vector", lambda e: e.tensor_tensor(out=S_R1[:, :], in0=S_R1[:, :],
                  in1=S_R2[:, :], op=ALU.mult), waits=[t_r], sem=sV)
        t_rbm = E_("tensor", lambda e: e.matmul(bank[6][:, :], ones_f[:, :],
                   S_R2[0:1, :], start=True, stop=True),
                   waits=[t_r, t_onf, ST["rb_war"]], sem=sT)
        t_rb = E_("vector", lambda e: e.tensor_copy(out=S_rb[:], in_=bank[6][:, :]),
                  waits=[t_rbm, pcV], sem=sV)
        ST["rb_war"] = t_rb
        t_cr = E_("vector", lambda e: e.tensor_tensor(out=S_cosr[:], in0=S_cosc[:],
                  in1=S_rb[:], op=ALU.mult), waits=[t_cos, t_rb, pcV], sem=sV)
        t_sr = E_("vector", lambda e: e.tensor_tensor(out=S_sinr[:], in0=S_sinc[:],
                  in1=S_rb[:], op=ALU.mult), waits=[t_sin, t_rb, pcV], sem=sV)
        ST["cos_war"] = t_sr
        for j in dout_list:
            pb = bank[ST["qkv_rot"] % 2]
            key = ST["qkv_rot"] % 2
            ST["qkv_rot"] += 1
            war = ST["qkv_war"].get(key)
            for dinc in range(DC):
                E_("tensor", lambda e, d=dinc, jj=j, p=pb, x=xb: e.matmul(
                    p[:, :], lhsT_for(jj, d), x[:, d * 512:(d + 1) * 512],
                    start=(d == 0), stop=False),
                    waits=[t_ld, war] if dinc == 0 else [])
            tok_mm = E_("tensor", lambda e, jj=j, p=pb: e.matmul(
                p[:, :], wsum_row[0:1, jj * 128:(jj + 1) * 128], S_R1[0:1, :],
                start=False, stop=True), waits=[t_mr], sem=sT)
            ST["qkv_war"][key] = evac(j, pb, tok_mm, t_cr, t_sr, t_rb)
        ST["pcT"] = (sT, sT.n)
        ST["pcV"] = (sV, sV.n)
        return t_ld

    def rope_evac(dst, psO, tok_mm, t_cr, t_sr):
        war = ST.get("rope_war")
        t_a = E_("vector", lambda e: e.tensor_tensor(out=S_t1[:], in0=psO[:, :],
                 in1=S_cosr[:], op=ALU.mult), waits=[tok_mm, t_cr, war], sem=sV)
        t_b = E_("vector", lambda e: e.tensor_tensor(out=S_t2[0:64, :], in0=psO[64:128, :],
                 in1=S_sinr[0:64, :], op=ALU.mult), waits=[tok_mm, t_sr, war], sem=sV)
        t_c = E_("vector", lambda e: e.tensor_tensor(out=S_t2[64:128, :], in0=psO[0:64, :],
                 in1=S_sinr[64:128, :], op=ALU.mult), waits=[tok_mm, t_sr, war], sem=sV)
        E_("vector", lambda e, d=dst: e.tensor_tensor(out=d[0:64, :], in0=S_t1[0:64, :],
           in1=S_t2[0:64, :], op=ALU.subtract), waits=[t_a, t_b], sem=sV)
        t_e = E_("vector", lambda e, d=dst: e.tensor_tensor(out=d[64:128, :],
                 in0=S_t1[64:128, :], in1=S_t2[64:128, :], op=ALU.add),
                 waits=[t_a, t_c], sem=sV)
        ST["rope_war"] = t_e
        return t_e

    wkv = carve(arS, 0, 4096)
    tok_wm = None
    for kv in range(2):
        for dinc in range(DC):
            tok_wm = E_("tensor", lambda e, d=dinc, k=kv: e.matmul(
                bank[4][0:1, k * 128:(k + 1) * 128], ones_b[:, :],
                wkv[:, (d * 2 + k) * 128:(d * 2 + k + 1) * 128],
                start=(d == 0), stop=(d == DC - 1)),
                waits=[t_wkv, t_onb], sem=(sT if dinc == DC - 1 else None))
    t_wsum = E_("vector", lambda e: e.tensor_scalar(
        out=S_wsum[:, :], in0=bank[4][0:1, 0:256], scalar1=-1.0,
        scalar2=None, op0=ALU.mult), waits=[tok_wm], sem=sV)
    ST["tr_war"] = t_wsum

    tok_xb_war = [None, None]
    for tc in range(DBG_NTC):
        sl = slice(tc * 512, (tc + 1) * 512)
        xb = xb_bufs[tc % 2]
        xdv = d_xT.rearrange("(c p) t -> p c t", p=128)[:, :, sl]

        def evac_kv(j, psO, tok_mm, t_cr, t_sr, t_rb, tc=tc):
            if j == 0:
                return rope_evac(K_fm[:, tc * 512:(tc + 1) * 512], psO, tok_mm, t_cr, t_sr)
            t_v = E_("vector", lambda e: e.tensor_tensor(
                out=S_vtmp[:], in0=psO[:, :], in1=S_rb[:], op=ALU.mult),
                waits=[tok_mm, t_rb, ST["v_war"]], sem=sV)
            ptr = bank[4].bitcast(BF16)
            ttr = None
            for i in range(4):
                ttr = E_("tensor", lambda e, ii=i: e.transpose(
                    ptr[:, ii * 128:(ii + 1) * 128],
                    S_vtmp[:, ii * 128:(ii + 1) * 128], idn_b[:, :]),
                    waits=[t_v, t_idb, ST["tr_war"]], sem=(sT if i == 3 else None))
            t_vc = E_("vector", lambda e: e.tensor_copy(
                out=V_tm[:, tc * 512:(tc + 1) * 512], in_=ptr[:, 0:512]),
                waits=[ttr], sem=sV)
            ST["v_war"] = t_vc
            ST["tr_war"] = t_vc
            return t_vc

        def lhsT_kv(j, dinc):
            return wkv[:, (dinc * 2 + j) * 128:(dinc * 2 + j + 1) * 128]

        a_chunk(xb, xdv, S_wsum, [0, 1], lhsT_kv, evac_kv,
                tok_xb_war[tc % 2], d_cosT[:, sl], d_sinT[:, sl], sXb[tc % 2])
        tok_xb_war[tc % 2] = (sT, sT.n)
    tok_akv_T = (sT, sT.n)

    for qh in range(DBG_AQ):
        t_wq = E_("gpsimd", lambda e, q0=qh: e.dma_start(
            out=waTq_sb, in_=d_waTq[:, q0 * 16384:(q0 + 1) * 16384]),
            waits=[(sT, sT.n)], sem=sWq, dma=True)
        prev_ts = (sV, sV.n)
        for jl in range(8):
            tok_wm = None
            for dinc in range(DC):
                tok_wm = E_("tensor", lambda e, d=dinc, j=jl: e.matmul(
                    bank[4][0:1, 0:128], ones_b[:, :],
                    waTq_sb[:, (j * DC + d) * 128:(j * DC + d + 1) * 128],
                    start=(d == 0), stop=(d == DC - 1)),
                    waits=[t_wq, prev_ts if dinc == 0 else None,
                           ST["tr_war"] if jl == 0 else None],
                    sem=(sT if dinc == DC - 1 else None))
            prev_ts = E_("vector", lambda e, j=jl: e.tensor_scalar(
                out=Q_wsum[:, j * 128:(j + 1) * 128], in0=bank[4][0:1, 0:128],
                scalar1=-1.0, scalar2=None, op0=ALU.mult),
                waits=[tok_wm, tok_akv_T], sem=sV)
        tok_qw_v = (sV, sV.n)

        for tq in range(NQC):
            sl = slice(tq * 512, (tq + 1) * 512)
            xb = xb_bufs[tq % 2]
            xdv = d_xTown.rearrange("(c p) t -> p c t", p=128)[:, :, sl]

            def evac_q(j, psO, tok_mm, t_cr, t_sr, t_rb, tq=tq, qh=qh):
                hg = qh * 8 + j
                dst = Q_sb[:, hg * TOK + tq * 512: hg * TOK + (tq + 1) * 512]
                return rope_evac(dst, psO, tok_mm, t_cr, t_sr)

            def lhsT_q(j, dinc):
                return waTq_sb[:, (j * DC + dinc) * 128:(j * DC + dinc + 1) * 128]

            if tq == 0:
                for key in (0, 1):
                    prev = ST["qkv_war"].get(key)
                    if prev is None or (prev[0] is tok_qw_v[0] and prev[1] < tok_qw_v[1]):
                        ST["qkv_war"][key] = tok_qw_v
            a_chunk(xb, xdv, Q_wsum, list(range(8)), lhsT_q, evac_q,
                    tok_xb_war[tq % 2], d_cosO[:, sl], d_sinO[:, sl], sXb[tq % 2])
            tok_xb_war[tq % 2] = (sT, sT.n)
    tok_aq_T = (sT, sT.n)
    tok_aq_V = (sV, sV.n)

    if dump == "A":
        _dump(EM, y_v, [(K_fm, 0, 8192), (V_tm, 8192, 8192),
                        (Q_sb[:, 0:4096], 16384, 4096),
                        (Q_sb[:, 15 * TOK:16 * TOK], 20480, 2048)],
              [tok_aq_T, tok_aq_V], sSt, sDmp,
              fitems=[(S_rb, 12), (S_cosr, 13), (S_sinr, 14), (S_cosc, 15)])
        _tail(EM, ALL_DMA_SEMS)
        return nc, EM

    ksq = carve(arS, 0, 8192)
    t_ksq = E_("vector", lambda e: e.tensor_tensor(out=ksq[:], in0=K_fm[:], in1=K_fm[:],
               op=ALU.mult), waits=[tok_aq_T, tok_aq_V], sem=sV)
    E_("vector", lambda e: e.memset(kmax[:], 0.0), sem=sV)
    E_("vector", lambda e: e.memset(qmax[:], 0.0), sem=sV)
    t_mx0 = (sV, sV.n)
    prev_red = t_mx0
    for j in range(16):
        pm = bank[j % 2]
        t_mm = E_("tensor", lambda e, jj=j, p=pm: e.matmul(
            p[0:1, :], ones_b[:, :], ksq[:, jj * 512:(jj + 1) * 512],
            start=True, stop=True), waits=[t_ksq, prev_red], sem=sT)
        t_red = E_("vector", lambda e, p=pm: e.tensor_reduce(
            out=sc1[:, :], in_=p[0:1, :], axis=AX.X, op=ALU.max),
            waits=[t_mm, prev_red], sem=sV)
        prev_red = E_("vector", lambda e: e.tensor_tensor(
            out=kmax[:, :], in0=kmax[:, :], in1=sc1[:, :], op=ALU.max),
            waits=[t_red], sem=sV)
    for piece in range(4):
        t_qsq = E_("vector", lambda e, pp=piece: e.tensor_tensor(
            out=ksq[:], in0=Q_sb[:, pp * 8192:(pp + 1) * 8192],
            in1=Q_sb[:, pp * 8192:(pp + 1) * 8192], op=ALU.mult),
            waits=[prev_red, (sT, sT.n)], sem=sV)
        for j in range(16):
            pm = bank[j % 2]
            t_mm = E_("tensor", lambda e, jj=j, p=pm: e.matmul(
                p[0:1, :], ones_b[:, :], ksq[:, jj * 512:(jj + 1) * 512],
                start=True, stop=True), waits=[t_qsq, prev_red], sem=sT)
            t_red = E_("vector", lambda e, p=pm: e.tensor_reduce(
                out=sc1[:, :], in_=p[0:1, :], axis=AX.X, op=ALU.max),
                waits=[t_mm, prev_red], sem=sV)
            prev_red = E_("vector", lambda e: e.tensor_tensor(
                out=qmax[:, :], in0=qmax[:, :], in1=sc1[:, :], op=ALU.max),
                waits=[t_red], sem=sV)
    t_l1 = E_("scalar", lambda e: e.activation(out=sc1[:, :], in_=kmax[:, :], func=AF.Ln),
              waits=[prev_red], sem=sA)
    t_l2 = E_("scalar", lambda e: e.activation(out=sc2[:, :], in_=qmax[:, :], func=AF.Ln),
              waits=[prev_red], sem=sA)
    t_ad = E_("vector", lambda e: e.tensor_tensor(out=sc1[:, :], in0=sc1[:, :],
              in1=sc2[:, :], op=ALU.add), waits=[t_l1, t_l2], sem=sV)
    t_ex = E_("scalar", lambda e: e.activation(out=sc1[:, :], in_=sc1[:, :],
              func=AF.Exp, scale=0.5), waits=[t_ad], sem=sA)
    t_ng = E_("vector", lambda e: e.tensor_scalar(out=sc1[:, :], in0=sc1[:, :],
              scalar1=-ISQ, scalar2=None, op0=ALU.mult), waits=[t_ex], sem=sV)
    t_bm = E_("tensor", lambda e: e.matmul(bank[2][:, 0:1], ones_f[:, :], sc1[0:1, 0:1],
              start=True, stop=True), waits=[t_ng, t_onf], sem=sT)
    t_bM = E_("vector", lambda e: e.tensor_copy(out=biasM[:, :], in_=bank[2][:, 0:1]),
              waits=[t_bm], sem=sV)

    out_fm = arB[:, :]
    EB2 = [carve(arS, i * 1024, (i + 1) * 1024) for i in range(6)]
    B_rb = carve(arS, 6144, 6656)
    B_acc = [carve(arS, 6656 + 2048 * i, 8704 + 2048 * i, F32) for i in range(2)]
    FT01 = [carve(arS, 10752 + 1024 * i, 11776 + 1024 * i) for i in range(2)]
    rcrow = arS[0:1, 13824:14848].bitcast(F32)
    PP = [pq01[:, :], pq23[:, :], pq45[:, :]]
    pavb = bank[0]
    psmb = bank[1]
    NPAIR = KCN // 2

    prev_last_av = t_bM
    prev_last_fold = t_bM
    prev_last_exp = t_bM
    prev_accread = t_bM
    prev_rb_read = t_bM
    prev_psm_read = t_bM
    slot_av = {}
    slot_fold = {}
    hq = 0
    for h in range(DBG_BH):
        for q in range(NQC):
            qsl = slice(h * TOK + q * 512, h * TOK + (q + 1) * 512)
            base_war = (sEp, hq) if hq >= 1 else None
            exp_toks = {}
            av_toks = {}
            facc = None
            facc_c = [None, None]
            for kp in range(NPAIR + 2):
                if kp < NPAIR:
                    w_sc = [exp_toks.get(kp - 3)]
                    if kp == 0:
                        w_sc += [tok_aq_V, t_bM]
                    if kp < 3:
                        w_sc.append(prev_last_exp)
                    E_("tensor", lambda e, k=kp, hs=qsl: e.matmul(
                        PP[k % 3][:, 0:512], K_fm[:, 2 * k * 128:(2 * k + 1) * 128],
                        Q_sb[:, hs], start=True, stop=True), waits=w_sc)
                    E_("tensor", lambda e, k=kp, hs=qsl: e.matmul(
                        PP[k % 3][:, 512:1024], K_fm[:, (2 * k + 1) * 128:(2 * k + 2) * 128],
                        Q_sb[:, hs], start=True, stop=True), sem=sT)
                    sc_tok = (sT, sT.n)
                    w_ex = [sc_tok, slot_av.get(kp % 6), slot_fold.get(kp % 6), t_bM]
                    exp_toks[kp] = E_("scalar", lambda e, k=kp: e.activation(
                        out=EB2[k % 6][:], in_=PP[k % 3][:, :], func=AF.Exp,
                        scale=ISQ, bias=biasM[:, 0:1]), waits=w_ex, sem=sA)
                j = kp - 2
                if 0 <= j < NPAIR:
                    E_("tensor", lambda e, k=j: e.matmul(
                        pavb, V_tm[:, 2 * k * 128:(2 * k + 1) * 128],
                        EB2[k % 6][:, 0:512],
                        start=(k == 0), stop=False, skip_group_check=True),
                        waits=[exp_toks[j]] + ([base_war] if j == 0 else []),
                        force=(j == 0))
                    av_toks[j] = E_("tensor", lambda e, k=j: e.matmul(
                        pavb, V_tm[:, (2 * k + 1) * 128:(2 * k + 2) * 128],
                        EB2[k % 6][:, 512:1024],
                        start=False, stop=(k == NPAIR - 1), skip_group_check=True),
                        sem=sT)
                    slot_av[j % 6] = av_toks[j]
                    if j % 2 == 1:
                        b = (j // 2) % 2
                        t_f0 = E_("vector", lambda e, k=j, bb=b: e.tensor_tensor(
                            out=FT01[bb][:], in0=EB2[(k - 1) % 6][:, :],
                            in1=EB2[k % 6][:, :], op=ALU.add),
                            waits=[exp_toks[j - 1], exp_toks[j], facc_c[b]], sem=sV)
                        if j < 4:
                            facc_c[b] = E_("vector", lambda e, bb=b: e.tensor_copy(
                                out=B_acc[bb][:], in_=FT01[bb][:]),
                                waits=[t_f0, prev_accread], sem=sV)
                        else:
                            facc_c[b] = E_("vector", lambda e, bb=b: e.tensor_tensor(
                                out=B_acc[bb][:], in0=B_acc[bb][:], in1=FT01[bb][:],
                                op=ALU.add), waits=[t_f0], sem=sV)
                        facc = facc_c[b]
                        slot_fold[(j - 1) % 6] = facc
                        slot_fold[j % 6] = facc
            E_("scalar", lambda e, hs=qsl: e.copy(
                out=out_fm[:, hs], in_=pavb),
                waits=[av_toks[NPAIR - 1]], sem=sEp)
            E_("tensor", lambda e: e.matmul(
                psmb[0:1, :], ones_fc[:, :], B_acc[0][:, 0:512],
                start=True, stop=False),
                waits=[facc_c[0], facc_c[1], prev_psm_read, t_onfc])
            E_("tensor", lambda e: e.matmul(
                psmb[0:1, :], ones_fc[:, :], B_acc[0][:, 512:1024],
                start=False, stop=False))
            E_("tensor", lambda e: e.matmul(
                psmb[0:1, :], ones_fc[:, :], B_acc[1][:, 0:512],
                start=False, stop=False))
            t_sum = E_("tensor", lambda e: e.matmul(
                psmb[0:1, :], ones_fc[:, :], B_acc[1][:, 512:1024],
                start=False, stop=True), sem=sT)
            prev_accread = t_sum
            t_rc = E_("vector", lambda e: e.reciprocal(
                out=rcrow[:, :], in_=psmb[0:1, :]), waits=[t_sum], sem=sV)
            t_bc = E_("tensor", lambda e: e.matmul(
                psmb[:, :], ones_f[:, :], rcrow[0:1, :], start=True, stop=True),
                waits=[t_rc], sem=sT)
            t_rb2 = E_("vector", lambda e: e.tensor_copy(
                out=B_rb[:], in_=psmb[:, :]), waits=[t_bc, prev_rb_read], sem=sV)
            tok_norm = E_("vector", lambda e, hs=qsl: e.tensor_tensor(
                out=out_fm[:, hs], in0=out_fm[:, hs], in1=B_rb[:], op=ALU.mult),
                waits=[t_rb2, (sEp, hq + 1)], sem=sV)
            tok_epilog = (sEp, hq + 1)
            prev_rb_read = tok_norm
            prev_psm_read = t_rb2
            prev_last_av = av_toks[NPAIR - 1]
            prev_last_fold = facc
            prev_last_exp = exp_toks[NPAIR - 1]
            hq += 1
    tok_b_done = (sV, sV.n)

    if dump == "B":
        _dump(EM, y_v, [(out_fm, 0, 32768)], [tok_b_done, (sT, sT.n)], sSt, sDmp)
        _tail(EM, ALL_DMA_SEMS)
        return nc, EM

    woT = [carve(arS, 0, 2048), carve(arS, 2048, 4096)]
    r2b = carve(arS, 4096, 5120, F32)
    m2b = carve(arS, 5120, 6144, F32)
    x1b_t = carve(arS, 6144, 6656)
    x1sq_t = carve(arS, 6656, 7168)
    xt_f = carve(arS, 7168, 8192, F32)
    aff_f = arS[0:16, 8192:9216].bitcast(F32)
    afftm = arS[:, 9216:9248].bitcast(F32)
    tk_m1 = arS[:, 9248:9250].bitcast(F32)
    tk_m2 = arS[:, 9250:9252].bitcast(F32)
    tk_eq1 = arS[:, 9252:9284].bitcast(F32)
    tk_eq2 = arS[:, 9284:9316].bitcast(F32)
    tk_a2 = arS[:, 9316:9348].bitcast(F32)
    tk_d = arS[:, 9348:9350].bitcast(F32)
    tk_t1 = arS[:, 9350:9382].bitcast(F32)
    tk_ctm = arS[:, 9382:9398]
    C_tmp = carve(arS, 13312, 14336, F32)
    C_R1 = arS[0:1, 10240:11264].bitcast(F32)
    C_R2 = arS[0:1, 11264:12288].bitcast(F32)
    C_R3 = arS[0:1, 12288:13312].bitcast(F32)
    x1v = [arC[:, :].bitcast(F32), arD[:, :].bitcast(F32)]

    pC = [bank[0], bank[1]]
    pS2, pQ2 = bank[2], bank[3]
    pR2 = bank[4]
    pAf = bank[5]
    pTr = bank[6]
    h2 = arA[:, :]

    tok_bT = (sT, sT.n)
    tok_woT_war = [[tok_b_done, tok_bT], [tok_b_done, tok_bT]]
    tok_x1_war = tok_bT
    tok_cd_last = None
    tok_d_apply = []
    for q in range(NQC):
        sum_tok = None
        sq_tok = None
        x1toks = {}
        st_toks = {}
        for dc in range(DC):
            idx = dc % 2
            wol = tok_woT_war[idx]
            t_wo = E_("gpsimd", lambda e, d=dc, i=idx: e.dma_start(
                out=woT[i], in_=d_woTt[d]),
                waits=(wol if isinstance(wol, list) else [wol]),
                sem=sWo[idx], dma=True)
            pb = pC[dc % 2]
            tok_mm = None
            for chc in range(NH):
                tok_mm = E_("tensor", lambda e, c=chc, p=pb, i=idx, qq=q: e.matmul(
                    p[:, :], woT[i][:, c * 128:(c + 1) * 128],
                    out_fm[:, c * TOK + qq * 512:c * TOK + (qq + 1) * 512],
                    start=(c == 0), stop=(c == NH - 1)),
                    waits=[t_wo, tok_b_done, x1toks.get(dc - 2)] if chc == 0 else [],
                    sem=(sT if chc == NH - 1 else None))
            tok_woT_war[idx] = tok_mm
            t_xt = E_("sync", lambda e, d=dc, qq=q: e.dma_start(
                out=xt_f, in_=d_xTown.rearrange("(c p) t -> p c t", p=128)
                [:, d, qq * 512:(qq + 1) * 512]), waits=[tok_x1_war], sem=sXt, dma=True)
            x1sl = x1v[dc // 8][:, (dc % 8) * 512:(dc % 8 + 1) * 512]
            t_x1 = E_("vector", lambda e, s=x1sl, p=pb: e.tensor_tensor(
                out=s, in0=p[:, :], in1=xt_f[:], op=ALU.add),
                waits=[tok_mm, t_xt, (sSt, sSt.n)], sem=sV)
            x1toks[dc] = t_x1
            tok_x1_war = t_x1
            st_toks[dc] = E_("sync", lambda e, s=x1sl, d=dc, qq=q: e.dma_start(
                out=d_x1s[d * 128:(d + 1) * 128, qq * 512:(qq + 1) * 512], in_=s),
                waits=[t_x1], sem=sSt, dma=True)
            t_xb = E_("vector", lambda e, s=x1sl: e.tensor_copy(out=x1b_t[:], in_=s),
                      waits=[t_x1, sq_tok], sem=sV)
            t_xq = E_("vector", lambda e: e.tensor_tensor(out=x1sq_t[:], in0=x1b_t[:],
                      in1=x1b_t[:], op=ALU.mult), waits=[t_xb], sem=sV)
            sum_tok = E_("tensor", lambda e, d=dc: e.matmul(
                pS2[0:1, :], ones_b[:, :], x1b_t[:],
                start=(d == 0), stop=(d == DC - 1)), waits=[t_xb], sem=sT)
            sq_tok = E_("tensor", lambda e, d=dc: e.matmul(
                pQ2[0:1, :], ones_b[:, :], x1sq_t[:],
                start=(d == 0), stop=(d == DC - 1)), waits=[t_xq], sem=sT)
        cdT, cdV = (sT, sT.n), (sV, sV.n)
        t_mean = E_("scalar", lambda e: e.mul(C_R1[:, :], pS2[0:1, :], 1.0 / 2048.0),
                    waits=[sum_tok, cdT, cdV], sem=sA)
        t_msq = E_("scalar", lambda e: e.mul(C_R2[:, :], pQ2[0:1, :], 1.0 / 2048.0),
                   waits=[sq_tok, cdT, cdV], sem=sA)
        t_m2 = E_("vector", lambda e: e.tensor_tensor(out=C_R3[:, :], in0=C_R1[:, :],
                  in1=C_R1[:, :], op=ALU.mult), waits=[t_mean], sem=sV)
        t_var = E_("vector", lambda e: e.tensor_tensor(out=C_R2[:, :], in0=C_R2[:, :],
                   in1=C_R3[:, :], op=ALU.subtract), waits=[t_msq, t_m2], sem=sV)
        t_varc = E_("vector", lambda e: e.tensor_scalar(out=C_R2[:, :], in0=C_R2[:, :],
                    scalar1=zero1[0:1, 0:1], scalar2=None, op0=ALU.max),
                    waits=[t_var], sem=sV)
        t_ln = E_("scalar", lambda e: e.activation(out=C_R2[:, :], in_=C_R2[:, :],
                  func=AF.Ln, bias=eps_t[0:1, 0:1]), waits=[t_varc], sem=sA)
        t_r = E_("scalar", lambda e: e.activation(out=C_R2[:, :], in_=C_R2[:, :],
                 func=AF.Exp, scale=-0.5), waits=[t_ln], sem=sA)
        t_mr = E_("vector", lambda e: e.tensor_tensor(out=C_R1[:, :], in0=C_R1[:, :],
                  in1=C_R2[:, :], op=ALU.mult), waits=[t_r], sem=sV)
        t_b1m = E_("tensor", lambda e: e.matmul(pR2[:, :], ones_f[:, :], C_R2[0:1, :],
                   start=True, stop=True), waits=[t_r], sem=sT)
        t_rb_ = E_("vector", lambda e: e.tensor_copy(out=r2b[:], in_=pR2[:, :]),
                   waits=[t_b1m, cdV], sem=sV)
        t_b2m = E_("tensor", lambda e: e.matmul(pR2[:, :], ones_f[:, :], C_R1[0:1, :],
                   start=True, stop=True), waits=[t_mr, t_rb_], sem=sT)
        t_mb_ = E_("vector", lambda e: e.tensor_copy(out=m2b[:], in_=pR2[:, :]),
                   waits=[t_b2m, cdV], sem=sV)
        apply_toks = []
        ap_war = None
        for dc in range(DC):
            x1sl = x1v[dc // 8][:, (dc % 8) * 512:(dc % 8 + 1) * 512]
            h2sl = h2[:, dc * TOK + q * 512:dc * TOK + (q + 1) * 512]
            t_a1 = E_("vector", lambda e, s=x1sl: e.tensor_tensor(
                out=C_tmp, in0=s, in1=r2b[:], op=ALU.mult),
                waits=[t_rb_, ap_war], sem=sV)
            t_ap = E_("vector", lambda e, hh=h2sl: e.tensor_tensor(
                out=hh, in0=C_tmp, in1=m2b[:], op=ALU.subtract),
                waits=[t_a1, t_mb_], sem=sV)
            ap_war = t_ap
            apply_toks.append(t_ap)
        tok_d_apply += apply_toks
        tok_aff = None
        for dinc in range(DC):
            tok_aff = E_("tensor", lambda e, d=dinc, qq=q: e.matmul(
                pAf[0:16, 0:512], wa_sb[:, d * NE:(d + 1) * NE],
                h2[:, d * TOK + qq * 512:d * TOK + (qq + 1) * 512],
                start=(d == 0), stop=(d == DC - 1)),
                waits=[apply_toks[dinc], t_wa], sem=(sT if dinc == DC - 1 else None))
        t_sg = E_("scalar", lambda e: e.activation(out=aff_f[:, :], in_=pAf[0:16, 0:512],
                  func=AF.Sigmoid), waits=[tok_aff], sem=sA)
        prev_ct = None
        for sub in range(4):
            t_tr = E_("tensor", lambda e, s=sub: e.transpose(
                pTr[:, 0:16], aff_f[0:16, s * 128:(s + 1) * 128], idn_f[0:16, 0:16]),
                waits=[t_sg, t_idf, prev_ct], sem=sT)
            t_at = E_("vector", lambda e: e.tensor_copy(out=afftm[:, :], in_=pTr[:, 0:16]),
                      waits=[t_tr, prev_ct], sem=sV)
            t_m1_ = E_("vector", lambda e: e.tensor_reduce(out=tk_m1[:, :], in_=afftm[:, :],
                       axis=AX.X, op=ALU.max), waits=[t_at], sem=sV)
            t_e1 = E_("vector", lambda e: e.tensor_scalar(out=tk_eq1[:, :], in0=afftm[:, :],
                      scalar1=tk_m1[:, 0:1], scalar2=None, op0=ALU.is_equal),
                      waits=[t_m1_], sem=sV)
            t_a2_ = E_("vector", lambda e: e.scalar_tensor_tensor(
                out=tk_a2[:, :], in0=tk_eq1[:, :], scalar=n1e9[:, 0:1], in1=afftm[:, :],
                op0=ALU.mult, op1=ALU.add), waits=[t_e1, t_n9], sem=sV)
            t_m2_ = E_("vector", lambda e: e.tensor_reduce(out=tk_m2[:, :], in_=tk_a2[:, :],
                       axis=AX.X, op=ALU.max), waits=[t_a2_], sem=sV)
            t_e2 = E_("vector", lambda e: e.tensor_scalar(out=tk_eq2[:, :], in0=tk_a2[:, :],
                      scalar1=tk_m2[:, 0:1], scalar2=None, op0=ALU.is_equal),
                      waits=[t_m2_], sem=sV)
            t_dn = E_("vector", lambda e: e.tensor_tensor(out=tk_d[:, :], in0=tk_m1[:, :],
                      in1=tk_m2[:, :], op=ALU.add), waits=[t_m2_], sem=sV)
            t_rd = E_("vector", lambda e: e.reciprocal(out=tk_d[:, :], in_=tk_d[:, :]),
                      waits=[t_dn], sem=sV)
            t_p1 = E_("vector", lambda e: e.tensor_tensor(out=tk_m1[:, :], in0=tk_m1[:, :],
                      in1=tk_d[:, :], op=ALU.mult), waits=[t_rd], sem=sV)
            t_p2 = E_("vector", lambda e: e.tensor_tensor(out=tk_m2[:, :], in0=tk_m2[:, :],
                      in1=tk_d[:, :], op=ALU.mult), waits=[t_rd], sem=sV)
            t_t1 = E_("vector", lambda e: e.tensor_scalar(out=tk_t1[:, :], in0=tk_eq1[:, :],
                      scalar1=tk_m1[:, 0:1], scalar2=None, op0=ALU.mult),
                      waits=[t_p1, t_e2], sem=sV)
            t_t2 = E_("vector", lambda e: e.tensor_scalar(out=tk_eq2[:, :], in0=tk_eq2[:, :],
                      scalar1=tk_m2[:, 0:1], scalar2=None, op0=ALU.mult),
                      waits=[t_p2], sem=sV)
            t_ct = E_("vector", lambda e: e.tensor_tensor(out=tk_ctm[:, 0:16], in0=tk_t1[:, :],
                      in1=tk_eq2[:, :], op=ALU.add), waits=[t_t1, t_t2], sem=sV)
            t_tb = E_("tensor", lambda e: e.transpose(
                pTr[0:16, :].bitcast(BF16)[0:16, 0:128], tk_ctm[:, 0:16], idn_b[:, :]),
                waits=[t_ct], sem=sT)
            prev_ct = E_("vector", lambda e, qq=q, s=sub: e.tensor_copy(
                out=c_fm[0:16, qq * 512 + s * 128:qq * 512 + (s + 1) * 128],
                in_=pTr[0:16, :].bitcast(BF16)[0:16, 0:128]),
                waits=[t_tb], sem=sV)
        tok_cd_last = prev_ct
    tok_d_done_v = (sV, sV.n)
    tok_d_done_t = (sT, sT.n)

    if dump == "CD":
        _dump(EM, y_v, [(h2[:, 0:16384], 0, 16384)],
              [tok_d_done_v, tok_d_done_t], sSt, sDmp,
              extra=[(c_fm[0:16, :], 16384, 2048)])
        _tail(EM, ALL_DMA_SEMS)
        return nc, EM

    accv = arB[:, :].bitcast(F32)
    y3ar = [arC[:, :], arD[:, :]]
    gblk = [carve(arS, 0, 2048), carve(arS, 2048, 4096)]
    w1blk = [carve(arS, 4096, 6144), carve(arS, 6144, 8192)]
    w2blk = [carve(arS, 8192, 10240), carve(arS, 10240, 12288)]
    ygt = [carve(arS, 12288, 12800), carve(arS, 12800, 13312)]
    cbt = [carve(arS, 13312, 13824), carve(arS, 13824, 14336)]
    c_row = arS[0:1, 14336:16384].bitcast(F32)
    x1f = [carve(arS, 8192, 10240, F32), carve(arS, 10240, 12288, F32)]

    pG = [bank[0], bank[1]]
    pL = [bank[2], bank[3]]
    pO = [bank[4], bank[6]]
    pCB = bank[5]

    tok_g_war = [tok_d_done_t, tok_d_done_t]
    tok_w1_war = [tok_d_done_t, tok_d_done_t]
    tok_w2_war = [tok_d_done_t, tok_d_done_t]
    tok_cd_st = (sSt, sSt.n)
    tok_y3_war = [[tok_d_done_t, tok_cd_st], [tok_d_done_t, tok_cd_st]]
    tok_pg_war = {}
    tok_po_war = {}
    tok_cbt_read = [t_bM, t_bM]
    tok_cb_pcb = t_bM
    tok_rb2_war = None
    for tc2 in range(2):
        tg = tc2 * 1024
        acc_init = [False] * (DC * 2)
        last_out_tok = None
        for ep in range(NEX // 2):
            for el in range(2):
                e_idx = 2 * ep + el
                cb_toks = [None, None]
                if e_idx < NE:
                    t_crow = E_("sync", lambda e, ee=e_idx, t0=tg: e.dma_start(
                        out=c_row[:, :], in_=c_fm[ee:ee + 1, t0:t0 + 1024]),
                        waits=[tok_cd_last, tok_cb_pcb], sem=sCrow, dma=True)
                    for th in range(2):
                        t_cm = E_("tensor", lambda e, h=th: e.matmul(
                            pCB[:, :], ones_f[:, :],
                            c_row[0:1, h * 512:(h + 1) * 512],
                            start=True, stop=True),
                            waits=[t_crow, tok_cb_pcb, tok_cbt_read[th]], sem=sT)
                        cb_toks[th] = E_("vector", lambda e, h=th: e.tensor_copy(
                            out=cbt[h][:], in_=pCB[:, :]), waits=[t_cm], sem=sV)
                        tok_cb_pcb = cb_toks[th]
                for fc in range(FC):
                    i = fc % 2
                    t_g = E_("gpsimd", lambda e, ee=e_idx, f=fc, i2=i: e.dma_start(
                        out=gblk[i2], in_=d_g[ee, f]), waits=[tok_g_war[i]],
                        sem=sG[i], dma=True)
                    t_w = E_("gpsimd", lambda e, ee=e_idx, f=fc, i2=i: e.dma_start(
                        out=w1blk[i2], in_=d_w1[ee, f]), waits=[tok_w1_war[i]],
                        sem=sW1[i], dma=True)
                    for th in range(2):
                        t0 = tg + th * 512
                        tokg = None
                        for dc in range(DC):
                            tokg = E_("tensor", lambda e, d=dc, i2=i, h=th, s=t0: e.matmul(
                                pG[h][:, :], gblk[i2][:, d * 128:(d + 1) * 128],
                                h2[:, d * TOK + s:d * TOK + s + 512],
                                start=(d == 0), stop=(d == DC - 1)),
                                waits=[t_g, tok_d_done_v,
                                       tok_pg_war.get(('g', th))] if dc == 0 else [],
                                sem=(sT if dc == DC - 1 else None))
                        tokl = None
                        for dc in range(DC):
                            tokl = E_("tensor", lambda e, d=dc, i2=i, h=th, s=t0: e.matmul(
                                pL[h][:, :], w1blk[i2][:, d * 128:(d + 1) * 128],
                                h2[:, d * TOK + s:d * TOK + s + 512],
                                start=(d == 0), stop=(d == DC - 1)),
                                waits=[t_w, tok_pg_war.get(('l', th))] if dc == 0 else [],
                                sem=(sT if dc == DC - 1 else None))
                        bcol = e_idx * FC + fc
                        t_yg = E_("scalar", lambda e, h=th, bc=bcol: e.activation(
                            out=ygt[h][:], in_=pG[h][:, :], func=AF.Gelu,
                            bias=gb_sb[:, bc:bc + 1]),
                            waits=[tokg, t_gb, tok_pg_war.get(('yg', th))], sem=sA)
                        tok_pg_war[('g', th)] = t_yg
                        y3sl = y3ar[el][:, fc * 1024 + th * 512:fc * 1024 + (th + 1) * 512]
                        t_y3 = E_("vector", lambda e, h=th, bc=bcol, s=y3sl: e.scalar_tensor_tensor(
                            out=s, in0=pL[h][:, :], scalar=b1_sb[:, bc:bc + 1],
                            in1=ygt[h][:], op0=ALU.add, op1=ALU.mult),
                            waits=[tokl, t_yg, t_b1c] + tok_y3_war[el], sem=sV)
                        tok_pg_war[('l', th)] = t_y3
                        tok_pg_war[('yg', th)] = t_y3
                        if e_idx < NE:
                            t_y3 = E_("vector", lambda e, h=th, s=y3sl: e.tensor_tensor(
                                out=s, in0=s, in1=cbt[h][:], op=ALU.mult),
                                waits=[t_y3, cb_toks[th]], sem=sV)
                            tok_cbt_read[th] = t_y3
                    tok_g_war[i] = (sT, sT.n)
                    tok_w1_war[i] = (sT, sT.n)
            tok_s1_done = (sV, sV.n)
            for dc in range(DC):
                i = dc % 2
                t_w2 = E_("gpsimd", lambda e, p=ep, d=dc, i2=i: e.dma_start(
                    out=w2blk[i2], in_=d_w2[p, d]), waits=[tok_w2_war[i]],
                    sem=sW2[i], dma=True)
                if ep == 0:
                    t_r2l = E_("sync", lambda e, d=dc: e.dma_start(
                        out=rb2_sb[:, :], in_=d_rb2[0:16, d * 128:(d + 1) * 128]),
                        waits=[tok_rb2_war], sem=sRb2, dma=True)
                    t_r2l2 = E_("sync", lambda e, d=dc: e.dma_start(
                        out=sb2_sb[:, :], in_=d_rb2[32:34, d * 128:(d + 1) * 128]),
                        waits=[tok_rb2_war], sem=sRb2, dma=True)
                for th in range(2):
                    nmm = 2 * FC + (2 if ep == 0 else 0)
                    k = 0
                    tok_o = None
                    for el in range(2):
                        for fc in range(FC):
                            first = (k == 0)
                            last = (k == nmm - 1)
                            w0 = [t_w2, tok_s1_done, tok_po_war.get(th),
                                  last_out_tok] if first else []
                            tok_o = E_("tensor", lambda e, l=el, f=fc, h=th, i2=i,
                                       st=first, sp=last: e.matmul(
                                pO[h][:, :],
                                w2blk[i2][:, (l * FC + f) * 128:(l * FC + f + 1) * 128],
                                y3ar[l][:, f * 1024 + h * 512:f * 1024 + (h + 1) * 512],
                                start=st, stop=sp, skip_group_check=True),
                                waits=w0, sem=(sT if last else None))
                            k += 1
                    if ep == 0:
                        tok_o = E_("tensor", lambda e, h=th, t0=tg: e.matmul(
                            pO[h][:, :], rb2_sb[:, :],
                            c_fm[0:16, t0 + h * 512:t0 + (h + 1) * 512],
                            start=False, stop=False, skip_group_check=True),
                            waits=[t_r2l2, tok_cd_last])
                        k += 1
                        tok_o = E_("tensor", lambda e, h=th: e.matmul(
                            pO[h][:, :], sb2_sb[:, :], ones2f[:, :],
                            start=False, stop=True, skip_group_check=True),
                            waits=[t_on2, t_r2l2], sem=sT)
                    accsl = accv[:, dc * 1024 + th * 512:dc * 1024 + (th + 1) * 512]
                    if not acc_init[dc * 2 + th]:
                        t_ac = E_("vector", lambda e, a=accsl, h=th: e.tensor_copy(
                            out=a, in_=pO[h][:, :]),
                            waits=[tok_o, tok_b_done,
                                   tok_po_war.get(('prev_tc', dc, th))], sem=sV)
                        acc_init[dc * 2 + th] = True
                    else:
                        t_ac = E_("vector", lambda e, a=accsl, h=th: e.tensor_tensor(
                            out=a, in0=a, in1=pO[h][:, :], op=ALU.add),
                            waits=[tok_o], sem=sV)
                    tok_po_war[th] = t_ac
                tok_w2_war[i] = (sT, sT.n)
                if ep == 0:
                    tok_rb2_war = (sT, sT.n)
            tok_y3_war = [[(sT, sT.n)], [(sT, sT.n)]]
        prev_x1 = [tok_g_war[0], tok_g_war[1]]
        fin_toks = {}
        for dc in range(DC):
            i = dc % 2
            t_xl = E_("sync", lambda e, d=dc, t0=tg, i2=i: e.dma_start(
                out=x1f[i2], in_=d_x1s[d * 128:(d + 1) * 128, t0:t0 + 1024]),
                waits=[prev_x1[i], (sT, sT.n) if dc < 2 else None,
                       (sSt, sSt.n) if dc < 2 else None], sem=sX1f[i], dma=True)
            accsl = accv[:, dc * 1024:(dc + 1) * 1024]
            t_fh = E_("vector", lambda e, a=accsl, d=dc, t0=tg: e.tensor_tensor(
                out=a, in0=a, in1=h2[:, d * TOK + t0:d * TOK + t0 + 1024],
                op=ALU.add),
                waits=[tok_po_war.get(0), tok_po_war.get(1)], sem=sV)
            t_fin = E_("vector", lambda e, a=accsl, i2=i: e.tensor_tensor(
                out=a, in0=a, in1=x1f[i2][:], op=ALU.add),
                waits=[t_xl, t_fh], sem=sV)
            prev_x1[i] = t_fin
            tok_st = E_("sync", lambda e, a=accsl, d=dc, t0=tg: e.dma_start(
                out=d_y[d * 128:(d + 1) * 128, t0:t0 + 1024], in_=a),
                waits=[t_fin], sem=sSt, dma=True)
            fin_toks[dc] = tok_st
            last_out_tok = tok_st
        for dc in range(DC):
            for th in range(2):
                tok_po_war[('prev_tc', dc, th)] = fin_toks[DC - 1]
        tok_g_war = [(sT, sT.n), (sT, sT.n)]
        tok_w1_war = [tok_g_war[0], tok_g_war[1]]
        tok_w2_war = [prev_x1[0], prev_x1[1]]
    _tail(EM, ALL_DMA_SEMS)
    return nc, EM


def _tail(EM, sems):
    for sm in sems:
        EM.step("sync", lambda e, s=sm: e.wait_ge(s.h, s.n), [])


def _dump(EM, y_v, items, waits, sSt, sDmp, extra=(), fitems=()):
    for ap, ci in fitems:
        n = ap.shape[-1]
        EM.step("sync", lambda e, a=ap, c=ci, nn=n: e.dma_start(
            out=y_v[:, c, 0:nn], in_=a), list(waits), sSt, True)
    for ap, col, n in items:
        c0, nch = col // 2048, n // 2048
        EM.step("gpsimd", lambda e, a=ap, c=c0, k=nch: e.dma_start(
            out=y_v[:, c:c + k, :],
            in_=a.rearrange("p (c t) -> p c t", c=k)), list(waits), sDmp, True)
    for ap, col, n in extra:
        c0, nch = col // 2048, n // 2048
        EM.step("gpsimd", lambda e, a=ap, c=c0, k=nch: e.dma_start(
            out=y_v[0:16, c:c + k, :],
            in_=a.rearrange("p (c t) -> p c t", c=k)), list(waits), sDmp, True)



def finalize(nc, EM):
    with nc.Block() as block:
        for name in ENGS:
            steps = EM.steps[name]

            def mk(steps):
                def body(eng):
                    for fn, ws, sem, amt in steps:
                        for (s, v) in ws:
                            eng.wait_ge(s.h, v)
                        ins = fn(eng)
                        if sem is not None and ins is not None:
                            ins.then_inc(sem.h, amt)
                return body
            getattr(block, name)(mk(steps))
    return nc



def _host_prep(inputs):
    x = np.asarray(inputs["L0_attn_x"], dtype=np.float32)
    w_attn = np.asarray(inputs["L0_attn_w_attn"], dtype=np.float32)
    w_out = np.asarray(inputs["L0_attn_w_out"], dtype=np.float32)
    wa = np.asarray(inputs["L0_ffn_wa"], dtype=np.float32)
    rg = np.asarray(inputs["L0_ffn_rg"], dtype=np.float32)
    rgb = np.asarray(inputs["L0_ffn_rgb"], dtype=np.float32)
    rw1 = np.asarray(inputs["L0_ffn_rw1"], dtype=np.float32)
    rb1 = np.asarray(inputs["L0_ffn_rb1"], dtype=np.float32)
    rw2 = np.asarray(inputs["L0_ffn_rw2"], dtype=np.float32)
    rb2 = np.asarray(inputs["L0_ffn_rb2"], dtype=np.float32)
    sg = np.asarray(inputs["L0_ffn_sg"], dtype=np.float32)
    sgb = np.asarray(inputs["L0_ffn_sgb"], dtype=np.float32)
    sw1 = np.asarray(inputs["L0_ffn_sw1"], dtype=np.float32)
    sb1 = np.asarray(inputs["L0_ffn_sb1"], dtype=np.float32)
    sw2 = np.asarray(inputs["L0_ffn_sw2"], dtype=np.float32)
    sb2 = np.asarray(inputs["L0_ffn_sb2"], dtype=np.float32)

    c = np.ascontiguousarray

    pos = np.asarray(10000.0 ** ((-2.0 * np.arange(0, HD, 2) - 1.0) / HD),
                     dtype=np.float32)
    angles = np.arange(T, dtype=np.float32)[:, None] * pos[None, :]
    rot = np.concatenate((angles, angles), axis=-1)
    cosT = c(np.cos(rot).astype(np.float32).T)
    sinT = c(np.sin(rot).astype(np.float32).T)

    waT = w_attn.T
    waTt = waT.reshape(DC, 128, NH + 2, 128).transpose(1, 0, 2, 3)
    waTkv = c(waTt[:, :, NH:, :].reshape(128, DC * 2 * 128))
    waTq = c(waTt[:, :, :NH, :].transpose(0, 2, 1, 3).reshape(128, NH * DC * 128))

    woT = w_out.T
    woTt = c(woT.reshape(NH, 128, DC, 128).transpose(2, 1, 0, 3).reshape(DC, 128, D))

    wa_t = c(wa.reshape(DC, 128, NE).transpose(1, 0, 2).reshape(128, DC * NE))

    A_g = np.concatenate([rg, sg], axis=0)
    A_w1 = np.concatenate([rw1, sw1], axis=0)
    A_w2 = np.concatenate([rw2, sw2], axis=0)
    g_t = c(A_g.reshape(NEX, DC, 128, FC, 128).transpose(0, 3, 2, 1, 4)
            .reshape(NEX, FC, 128, DC * 128))
    w1_t = c(A_w1.reshape(NEX, DC, 128, FC, 128).transpose(0, 3, 2, 1, 4)
             .reshape(NEX, FC, 128, DC * 128))
    w2t = A_w2.reshape(NEX, FC, 128, DC, 128).transpose(0, 3, 2, 1, 4)
    w2p = c(w2t.reshape(NEX // 2, 2, DC, 128, FC * 128)
            .transpose(0, 2, 3, 1, 4).reshape(NEX // 2, DC, 128, 2048))

    gball = np.concatenate([rgb, sgb], axis=0)
    b1all = np.concatenate([rb1, sb1], axis=0)
    gb_t = c(gball.reshape(NEX, FC, 128).transpose(2, 0, 1).reshape(128, NEX * FC))
    b1_t = c(b1all.reshape(NEX, FC, 128).transpose(2, 0, 1).reshape(128, NEX * FC))

    rb2sb2 = np.zeros((34, D), dtype=np.float32)
    rb2sb2[0:16] = rb2
    rb2sb2[32:34] = sb2

    idn = np.eye(128, dtype=np.float32)

    shared = dict(cosT=cosT, sinT=sinT, waTkv=waTkv, waTq=waTq, woTt=woTt,
                  wa=wa_t, g_t=g_t, w1_t=w1_t, w2p_t=w2p, gb_t=gb_t, b1_t=b1_t,
                  rb2sb2=rb2sb2, idn=idn)

    xTb = [c(x[b].T) for b in range(B)]
    in_maps = []
    for core in range(8):
        b = core // 4
        q0 = (core % 4) * TOK
        m = dict(shared)
        m["xT"] = xTb[b]
        m["xTown"] = c(xTb[b][:, q0:q0 + TOK])
        m["cosO"] = c(cosT[:, q0:q0 + TOK])
        m["sinO"] = c(sinT[:, q0:q0 + TOK])
        in_maps.append(m)
    return in_maps


def kernel(**inputs):
    in_maps = _host_prep(inputs)
    nc, EM = build_program(dump=None)
    nc = finalize(nc, EM)
    res = run_bass_kernel_spmd(nc, in_maps, core_ids=list(range(8)))
    out = np.empty((B, T, D), dtype=np.float32)
    for core in range(8):
        b = core // 4
        q0 = (core % 4) * TOK
        out[b, q0:q0 + TOK, :] = res.results[core]["y"].T
    return out
